# version 15
# baseline (speedup 1.0000x reference)
import math
from contextlib import ExitStack

import numpy as np
import ml_dtypes

import concourse.bass as bass
import concourse.mybir as mybir
from concourse.bass_utils import run_bass_kernel_spmd

F32 = mybir.dt.float32
BF16 = mybir.dt.bfloat16
AF = mybir.ActivationFunctionType
ALU = mybir.AluOpType

D = 2048
N = 2048
NT = 16
H = 16
INTER = 8192
ITI = 64
NCORES = 8
EPS_LN = 1e-5
EPS_ATTN = 1e-6
RS2048 = 1.0 / math.sqrt(2048.0)
RS8192 = 1.0 / math.sqrt(8192.0)

ENGS = ("sync", "gp", "sc", "ve", "te")


class Sem:
    def __init__(self, h):
        self.h = h
        self.c = 0

    def nxt(self, n=1):
        self.c += n
        return self.c


class Prog:
    def __init__(self):
        self.streams = {k: [] for k in ENGS}

    def em(self, eng, fn):
        self.streams[eng].append(fn)

    def wait(self, eng, tok):
        if tok is None:
            return
        h, v = tok
        self.em(eng, lambda e, h=h, v=v: e.wait_ge(h, v))


class DmaQ:

    def __init__(self, prog, eng, sems):
        self.p = prog
        self.eng = eng
        self.sems = sems
        self.i = 0

    def issue(self, fn, pre_waits=()):
        s = self.sems[self.i % len(self.sems)]
        self.i += 1
        if s.c > 0:
            self.p.wait(self.eng, (s.h, s.c))
        for t in pre_waits:
            self.p.wait(self.eng, t)
        v = s.nxt(16)
        self.p.em(self.eng, lambda e, fn=fn, h=s.h: fn(e).then_inc(h, 16))
        return (s.h, v)


def build_nc(taps=()):
    nc = bass.Bass("TRN2", target_bir_lowering=False)

    xT = nc.dram_tensor("xT", [D, N], F32, kind="ExternalInput").ap()
    wqT = nc.dram_tensor("wqT", [D, D], F32, kind="ExternalInput").ap()
    wkTa = nc.dram_tensor("wkTa", [D + 1, D], F32, kind="ExternalInput").ap()
    wvTa = nc.dram_tensor("wvTa", [D + 1, D], F32, kind="ExternalInput").ap()
    woT = nc.dram_tensor("woT", [D, D], F32, kind="ExternalInput").ap()
    wupT = nc.dram_tensor("wupT", [D, INTER], F32, kind="ExternalInput").ap()
    wgT = nc.dram_tensor("wgT", [D, INTER], F32, kind="ExternalInput").ap()
    wd_slice = nc.dram_tensor("wd_slice", [INTER, 256], F32,
                              kind="ExternalInput").ap()
    pp_in = nc.dram_tensor("pp_in", [128, 208], F32, kind="ExternalInput").ap()
    sd_in = nc.dram_tensor("sd_in", [128, 64], F32, kind="ExternalInput").ap()
    consts_in = nc.dram_tensor("consts_in", [128, 2308], BF16,
                               kind="ExternalInput").ap()
    decay_in = nc.dram_tensor("decay_in", [128, N], BF16,
                              kind="ExternalInput").ap()

    yT = nc.dram_tensor("yT", [D, N], F32, kind="ExternalOutput").ap()

    wdc = nc.dram_tensor("wdc", [16 * 64 * 128, 128], BF16, kind="Internal")
    wd_contrib = nc.dram_tensor("wd_contrib", [2 * 64 * 128, 128], BF16,
                                kind="Internal")
    h_dram = nc.dram_tensor("h_dram", [INTER, N], BF16, kind="Internal").ap()
    xb_dram = nc.dram_tensor("xb_dram", [D, N], F32, kind="Internal").ap()
    bcd = nc.dram_tensor("bcd", [4, N], F32, kind="Internal").ap()
    cc_in = nc.dram_tensor("cc_kv_in", [128, H * 129], F32, kind="Internal")
    cc_out = nc.dram_tensor("cc_kv_out", [128, H * 129], F32, kind="Internal")

    tap_out = {}
    for tname, shape, dt in (
        ("t_x1T", [128, NT * 2048], BF16), ("t_K", [128, NT * 2048], BF16),
        ("t_KV", [128, H * 129], F32), ("t_QT", [128, NT * 2048], BF16),
        ("t_attnT", [128, NT * 2048], BF16),
        ("t_out2T", [128, NT * 2048], BF16),
        ("t_xB", [D, N], F32), ("t_x3sT", [128, NT * 2048], BF16),
        ("t_fT", [128, NT * 2048], BF16), ("t_h", [INTER, N], BF16),
        ("t_x5", [128, NT * 512], BF16), ("t_wdc", [16384, 128], BF16),
    ):
        if tname in taps:
            tap_out[tname] = nc.dram_tensor(tname, shape, dt,
                                            kind="ExternalOutput").ap()

    P = Prog()
    em, wait = P.em, P.wait

    with ExitStack() as ctx:
        sb = lambda nm, shape, dt: ctx.enter_context(nc.sbuf_tensor(nm, shape, dt))
        X = sb("Xb", [128, NT * 2048], BF16)
        Y = sb("Yb", [128, NT * 2048], BF16)
        XT = sb("XTb", [128, 4096], F32)
        WBL = sb("WBL", [128, 8192], BF16)
        WB = [WBL.ap()[:, 0:4096], WBL.ap()[:, 4096:8192],
              sb("WB2t", [128, 4096], BF16)]
        BC = sb("BCs", [128, 2048], BF16)
        BC2 = sb("BCm", [128, 2048], BF16)
        DKB = sb("DKB", [128, 2080], BF16)
        ROWS = sb("ROWS", [1, 1024], F32)
        ZALL = sb("ZALL", [16, 2048], BF16)
        KSB = sb("KSB", [128, 256], BF16)
        BROW = sb("BROW", [1, 1024], BF16)
        VE2 = sb("VE2", [128, 1040], BF16)
        CONSTS = sb("CONSTS", [128, 2308], BF16)
        PP = sb("PP", [128, 208], F32)
        OST = sb("OST", [128, 2080], F32)
        psP = [ctx.enter_context(nc.psum_tensor(f"psP{i}", [128, 512], F32))
               for i in range(8)]

        H128 = CONSTS[:, 0:128]
        ONESC = CONSTS[:, 128:129]
        ONESR = CONSTS[0:1, 129:257]
        INDH = lambda h: CONSTS[0:16, 260 + h * 128:260 + (h + 1) * 128]

        WBLF = WBL.ap().bitcast(F32)
        WB_F = [WBLF[:, 0:2048], WBLF[:, 2048:4096],
                WB[2].ap().bitcast(F32)]
        TMPF = WB_F[0]
        UGE = XT.ap().bitcast(BF16)
        UGEHI = UGE[:, 4096:8192]

        mksem = lambda nm: Sem(ctx.enter_context(nc.semaphore(nm)))
        SQ = DmaQ(P, "sync", [mksem(f"sq{i}") for i in range(10)])
        GQ = DmaQ(P, "gp", [mksem(f"gq{i}") for i in range(10)])
        TE, VE, SC, GP, CCS, AGS = (mksem(n) for n in
                                    ("teP", "veP", "scP", "gpP", "ccS", "agS"))

        def mk(sem, eng):
            def f(fn):
                v = sem.nxt()
                em(eng, lambda e, fn=fn: fn(e).then_inc(sem.h, 1))
                return (sem.h, v)
            return f
        te, ve, sc, gp = mk(TE, "te"), mk(VE, "ve"), mk(SC, "sc"), mk(GP, "gp")
        te_ = lambda fn: em("te", fn)
        ve_ = lambda fn: em("ve", fn)
        sc_ = lambda fn: em("sc", fn)

        Xt = lambda t: X[:, t * 2048:(t + 1) * 2048]
        Yt = lambda t: Y[:, t * 2048:(t + 1) * 2048]
        Xq = lambda i: X[:, i * 512:(i + 1) * 512]

        store_toks = []

        OROW = OST.ap()[0:1, 0:1024]

        def ln_rows(ps1_ap, ps2_ap, chunk_slot, stat_tok, prev_dma,
                    denom=2048.0, rb=None):
            lo = chunk_slot
            R = ROWS.ap() if rb is None else rb
            wait("ve", stat_tok)
            for t in prev_dma:
                wait("ve", t)
            ve_(lambda e: e.tensor_scalar_mul(R[0:1, 0:512], ps1_ap,
                                              1.0 / denom))
            vm = ve(lambda e: e.tensor_scalar_mul(R[0:1, 512:1024], ps2_ap,
                                                  1.0 / denom))
            tm = SQ.issue(lambda e: e.dma_start(
                out=bcd[1:2, lo:lo + 512], in_=R[0:1, 0:512]),
                pre_waits=[vm])
            wait("sc", tm)
            sq_t = sc(lambda e: e.activation(R[0:1, 0:512], R[0:1, 0:512],
                                             AF.Square))
            wait("ve", sq_t)
            ve_(lambda e: e.tensor_sub(R[0:1, 512:1024], R[0:1, 512:1024],
                                       R[0:1, 0:512]))
            va = ve(lambda e: e.tensor_scalar_add(R[0:1, 512:1024],
                                                  R[0:1, 512:1024], EPS_LN))
            wait("sc", va)
            sc_(lambda e: e.activation(R[0:1, 512:1024], R[0:1, 512:1024],
                                       AF.Ln))
            si = sc(lambda e: e.activation(R[0:1, 512:1024],
                                           R[0:1, 512:1024], AF.Exp,
                                           scale=-0.5))
            ts = SQ.issue(lambda e: e.dma_start(
                out=bcd[0:1, lo:lo + 512], in_=R[0:1, 512:1024]),
                pre_waits=[si])
            b0 = GQ.issue(lambda e: e.dma_start(
                out=BC[:, lo:lo + 512],
                in_=bcd[0:1, lo:lo + 512].partition_broadcast(128)),
                pre_waits=[ts])
            b1 = GQ.issue(lambda e: e.dma_start(
                out=BC2[:, lo:lo + 512],
                in_=bcd[1:2, lo:lo + 512].partition_broadcast(128)),
                pre_waits=[tm])
            return b0, b1, tm, ts, vm

        tok_pp = SQ.issue(lambda e: e.dma_start(out=PP[:, :], in_=pp_in))
        tok_sd = SQ.issue(lambda e: e.dma_start(
            out=VE2.ap().bitcast(F32)[:, 0:64], in_=sd_in))
        tok_cst = SQ.issue(lambda e: e.dma_start(out=CONSTS[:, :], in_=consts_in))
        tok_dk = SQ.issue(lambda e: e.dma_start(out=DKB[:, 0:2048], in_=decay_in))
        wait("ve", tok_pp)
        wait("ve", tok_sd)
        ve_(lambda e: e.tensor_mul(PP[:, 64:128], PP[:, 64:128],
                                   VE2.ap().bitcast(F32)[:, 0:64]))
        ve_(lambda e: e.tensor_scalar_mul(PP[:, 64:128], PP[:, 64:128], RS2048))
        ve_(lambda e: e.tensor_scalar_mul(PP[:, 128:192], PP[:, 128:192], RS2048))
        tok_scales = ve(lambda e: e.tensor_scalar_mul(PP[:, 192:208],
                                                      PP[:, 192:208], RS8192))
        tok_ones = None
        for s in (0, 1):
            vslot = VE2[:, s * 520:s * 520 + 516].rearrange(
                "p (j f) -> p j f", j=4)
            tok_ones = ve(lambda e, vslot=vslot: e.memset(
                vslot[:, :, 128:129], 1.0))

        psS1 = lambda c: psP[c][0:1, 0:512]
        psS2 = lambda c: psP[4 + c][0:1, 0:512]
        last_stat = None
        for t in range(NT):
            ld = GQ.issue(lambda e, t=t: e.dma_start(
                out=Xt(t), in_=xT[t * 128:(t + 1) * 128, :]))
            wait("sc", ld)
            sq_tok = sc(lambda e, t=t: e.activation(Yt(t), Xt(t), AF.Square))
            if t == 0:
                wait("te", tok_cst)
            wait("te", ld)
            wait("te", sq_tok)
            for c in range(4):
                te_(lambda e, t=t, c=c: e.matmul(
                    psS1(c), ONESC, Xt(t)[:, c * 512:(c + 1) * 512],
                    start=(t == 0), stop=(t == NT - 1)))
                last_stat = te(lambda e, t=t, c=c: e.matmul(
                    psS2(c), ONESC, Yt(t)[:, c * 512:(c + 1) * 512],
                    start=(t == 0), stop=(t == NT - 1)))
        bc1 = []
        pdma = {0: [], 1: []}
        for c in range(4):
            b0, b1, tm, ts, _ = ln_rows(psS1(c), psS2(c), c * 512,
                                        last_stat, pdma[c % 2],
                                        rb=(None if c % 2 == 0 else OROW))
            pdma[c % 2] = [tm, ts]
            bc1 += [b0, b1]

        x1_toks = [None] * NT
        for t in range(NT):
            sl = t % 2
            pre = [x1_toks[t - 2]] if t >= 2 else []
            ld = SQ.issue(lambda e, t=t, sl=sl: e.dma_start(
                out=XT[:, sl * 2048:(sl + 1) * 2048],
                in_=xT[t * 128:(t + 1) * 128, :]), pre_waits=pre)
            tmp = TMPF if t % 2 == 0 else OST.ap()[:, 0:2048]
            if sl == 0:
                wait("gp", ld)
                if t == 0:
                    for b in bc1:
                        wait("gp", b)
                if t >= 2:
                    wait("gp", x1_toks[t - 2])
                gsub = gp(lambda e, sl=sl, tmp=tmp: e.tensor_sub(
                    tmp, XT[:, sl * 2048:(sl + 1) * 2048], BC2[:, :]))
                wait("ve", gsub)
            else:
                wait("ve", ld)
                if t == 1:
                    for b in bc1:
                        wait("ve", b)
                ve_(lambda e, sl=sl, tmp=tmp: e.tensor_sub(
                    tmp, XT[:, sl * 2048:(sl + 1) * 2048], BC2[:, :]))
            x1_toks[t] = ve(lambda e, t=t, tmp=tmp: e.tensor_mul(
                Xt(t), tmp, BC[:, :]))
        x1_done = x1_toks[NT - 1]
        if "t_x1T" in tap_out:
            store_toks.append(SQ.issue(lambda e: e.dma_start(
                out=tap_out["t_x1T"], in_=X[:, :]), pre_waits=[x1_done]))

        psA, psB = psP[0], psP[1]
        CH = [WBL.ap(), UGE[:, 0:8192]]

        def rm_phase512(wsrc, evac_fn, init_guards, kv_hook=None):
            ev = {}
            last_mm = {}
            for dc4 in range(4):
                buf = CH[dc4 % 2]
                bs = (dc4 % 2) * 512
                pre = list(init_guards[dc4 % 2]) if dc4 < 2 else \
                    [last_mm[dc4 - 2]]
                wtok = GQ.issue(lambda e, dc4=dc4, buf=buf: e.dma_start(
                    out=buf.rearrange("p (k f) -> p k f", k=16),
                    in_=wsrc[0:2048, dc4 * 512:(dc4 + 1) * 512].rearrange(
                        "(k p) f -> p k f", p=128)), pre_waits=pre)
                btok = GQ.issue(lambda e, dc4=dc4, bs=bs: e.dma_start(
                    out=BROW[0:1, bs:bs + 512],
                    in_=wsrc[2048:2049, dc4 * 512:(dc4 + 1) * 512]),
                    pre_waits=(list(init_guards[dc4 % 2]) if dc4 < 2
                               else [last_mm[dc4 - 2]]))
                for nt in range(NT):
                    ps = psA if nt % 2 == 0 else psB
                    if nt < 2:
                        wait("te", wtok)
                        wait("te", btok)
                    if dc4 == 0 and nt == 0:
                        wait("te", x1_done)
                    pk = ev.get((dc4, nt - 2)) if nt >= 2 else \
                        (ev.get((dc4 - 1, NT - 2 + nt)) if dc4 > 0 else None)
                    if pk is not None:
                        wait("te", pk)
                    for k in range(NT):
                        te_(lambda e, k=k, nt=nt, buf=buf, ps=ps: e.matmul(
                            ps[:, 0:512],
                            Xt(k)[:, nt * 128:(nt + 1) * 128],
                            buf[:, k * 512:(k + 1) * 512],
                            start=(k == 0), stop=False))
                    mm = te(lambda e, bs=bs, ps=ps: e.matmul(
                        ps[:, 0:512], ONESR, BROW[0:1, bs:bs + 512],
                        start=False, stop=True))
                    last_mm[dc4] = mm
                    ev[(dc4, nt)] = evac_fn(dc4, nt, ps, mm)
                    if kv_hook is not None:
                        kv_hook(dc4, nt, ev[(dc4, nt)])
            return ev, last_mm

        def k_evac(dc4, nt, ps, mm):
            wait("sc", mm)
            if dc4 == 0 and nt == 0:
                wait("sc", tok_pp)
            return sc(lambda e, dc4=dc4, nt=nt, ps=ps: e.activation(
                Yt(nt)[:, dc4 * 512:(dc4 + 1) * 512], ps[:, 0:512],
                AF.Relu, scale=PP[:, 48 + nt:49 + nt]))

        k_ev, k_last = rm_phase512(wkTa, k_evac, [[x1_done], [x1_done]])
        if "t_K" in tap_out:
            store_toks.append(SQ.issue(lambda e: e.dma_start(
                out=tap_out["t_K"], in_=Y[:, :]), pre_waits=[k_ev[(3, 15)]]))

        psKV = [psP[2][:, 0:129], psP[3][:, 0:129],
                psP[4][:, 0:129], psP[5][:, 0:129]]
        kv_mm = {}
        kv_cp = {}

        def v_evac(dc4, nt, ps, mm):
            wait("sc", mm)
            sslot = nt % 2
            pk = kv_mm.get((dc4, nt - 2)) if nt >= 2 else \
                (kv_mm.get((dc4 - 1, NT - 2 + nt)) if dc4 > 0 else None)
            if pk is not None:
                wait("sc", pk)
            if dc4 == 0 and nt == 0:
                wait("sc", tok_ones)
            dst = VE2[:, sslot * 520:sslot * 520 + 516].rearrange(
                "p (j f) -> p j f", j=4)[:, :, 0:128]
            src = ps.ap()[:, 0:512].rearrange(
                "p (j f) -> p j f", j=4)[:, :, 0:128]
            return sc(lambda e, dst=dst, src=src: e.activation(
                dst, src, AF.Copy))

        def kv_hook(dc4, nt, evtok):
            sslot = nt % 2
            tok = None
            for j in range(4):
                hd = 4 * dc4 + j
                wait("te", evtok)
                wait("te", k_ev[(dc4, nt)])
                if nt == 0 and dc4 > 0:
                    pc = kv_cp.get((dc4 - 1, j))
                    if pc is not None:
                        wait("te", pc)
                tok = te(lambda e, j=j, sslot=sslot, hd=hd, nt=nt: e.matmul(
                    psKV[j], Yt(nt)[:, hd * 128:(hd + 1) * 128],
                    VE2[:, sslot * 520 + j * 129:sslot * 520 + j * 129 + 129],
                    start=(nt == 0), stop=(nt == NT - 1)))
            kv_mm[(dc4, nt)] = tok
            if nt == NT - 1:
                for j in range(4):
                    hd = 4 * dc4 + j
                    wait("ve", tok)
                    kv_cp[(dc4, j)] = ve(lambda e, j=j, hd=hd: e.tensor_copy(
                        OST[:, hd * 129:(hd + 1) * 129], psKV[j]))

        v_ev, v_last = rm_phase512(wvTa, v_evac, [[k_last[2]], [k_last[3]]],
                                   kv_hook=kv_hook)
        kv_done = kv_cp[(3, 3)]

        kv_out = SQ.issue(lambda e: e.dma_start(
            out=cc_in.ap(), in_=OST.ap()[:, 0:H * 129]), pre_waits=[kv_done])
        wait("gp", kv_out)
        ccv = CCS.nxt()
        em("gp", lambda e: e.collective_compute(
            "AllReduce", ALU.add, replica_groups=[list(range(NCORES))],
            ins=[cc_in.ap().opt()], outs=[cc_out.ap().opt()]
        ).then_inc(CCS.h, 1))
        cc_tok = (CCS.h, ccv)
        if "t_KV" in tap_out:
            store_toks.append(SQ.issue(lambda e: e.dma_start(
                out=tap_out["t_KV"], in_=OST.ap()[:, 0:H * 129]),
                pre_waits=[kv_done]))

        q_ev = {}
        q_sc = {}
        q_last = {}
        vlast = None
        for dc4 in range(4):
            buf = CH[dc4 % 2]
            pre = [v_last[2 + dc4]] if dc4 < 2 else [q_last[dc4 - 2]]
            qch = GQ.issue(lambda e, dc4=dc4, buf=buf: e.dma_start(
                out=buf.rearrange("p (k f) -> p k f", k=16),
                in_=wqT[:, dc4 * 512:(dc4 + 1) * 512].rearrange(
                    "(k p) f -> p k f", p=128)), pre_waits=pre)
            for dtl in range(4):
                dt = 4 * dc4 + dtl
                for c in range(4):
                    idx = dt * 4 + c
                    ps = psA if idx % 2 == 0 else psB
                    if dtl == 0 and c == 0:
                        wait("te", qch)
                    pk = q_sc.get(idx - 2)
                    if pk is not None:
                        wait("te", pk)
                    for k in range(NT - 1):
                        te_(lambda e, k=k, buf=buf, dtl=dtl, c=c, ps=ps:
                            e.matmul(
                                ps[:, 0:512],
                                buf[:, k * 512 + dtl * 128:
                                    k * 512 + (dtl + 1) * 128],
                                Xt(k)[:, c * 512:(c + 1) * 512],
                                start=(k == 0), stop=False))
                    mmq = te(lambda e, buf=buf, dtl=dtl, c=c, ps=ps: e.matmul(
                        ps[:, 0:512],
                        buf[:, (NT - 1) * 512 + dtl * 128:
                            (NT - 1) * 512 + (dtl + 1) * 128],
                        Xt(NT - 1)[:, c * 512:(c + 1) * 512],
                        start=False, stop=True))
                    q_last[dc4] = mmq
                    oslot = (idx % 2) * 512
                    wait("sc", mmq)
                    pv = q_ev.get(divmod(idx - 2, 4)) if idx >= 2 else None
                    if pv is not None:
                        wait("sc", pv)
                    if idx == 0:
                        wait("sc", kv_out)
                    q_sc[idx] = sc(lambda e, dt=dt, oslot=oslot, ps=ps:
                                   e.activation(
                                       OST[:, oslot:oslot + 512],
                                       ps[:, 0:512], AF.Relu,
                                       bias=PP[:, 0 + dt:1 + dt]))
                    wait("ve", q_sc[idx])
                    if idx == 0:
                        wait("ve", tok_dk)
                    q_ev[(dt, c)] = ve(lambda e, dt=dt, c=c, oslot=oslot:
                                       e.tensor_mul(
                                           Yt(dt)[:, c * 512:(c + 1) * 512],
                                           OST.ap().bitcast(F32)[
                                               :, oslot:oslot + 512]
                                           if False else
                                           OST[:, oslot:oslot + 512],
                                           DKB[:, c * 512:(c + 1) * 512]))
                    vlast = q_ev[(dt, c)]
        q_done = vlast
        if "t_QT" in tap_out:
            store_toks.append(SQ.issue(lambda e: e.dma_start(
                out=tap_out["t_QT"], in_=Y[:, :]), pre_waits=[q_done]))

        wait("gp", cc_tok)
        kvbf = GQ.issue(lambda e: e.dma_start(
            out=DKB[:, 0:H * 129], in_=cc_out.ap()), pre_waits=[q_done])

        psDen = lambda c: psP[4 + c][0:16, 0:512]
        gp_ksb = gp(lambda e: e.memset(KSB[:, :], 0.0))
        wait("sc", gp_ksb)
        wait("sc", kvbf)
        ksb_tok = None
        for k in range(NT):
            ksb_tok = sc(lambda e, k=k: e.activation(
                KSB[:, k * 16 + k:k * 16 + k + 1],
                DKB[:, k * 129 + 128:k * 129 + 129], AF.Copy))
        den_t = {}
        for c in range(4):
            for k in range(NT):
                if c == 0 and k == 0:
                    wait("te", ksb_tok)
                    wait("te", kvbf)
                den_t[c] = te(lambda e, c=c, k=k: e.matmul(
                    psDen(c), KSB[:, k * 16:(k + 1) * 16],
                    Yt(k)[:, c * 512:(c + 1) * 512],
                    start=(k == 0), stop=(k == NT - 1)))
        z_t = {}
        for c in range(4):
            wait("ve", den_t[c])
            ve_(lambda e, c=c: e.tensor_scalar_add(psDen(c), psDen(c),
                                                   EPS_ATTN))
            ve_(lambda e, c=c: e.reciprocal(psDen(c), psDen(c)))
            z_t[c] = ve(lambda e, c=c: e.tensor_copy(
                ZALL[:, c * 512:(c + 1) * 512], psDen(c)))
        att = {}
        zb_t = {}
        for hd in range(H):
            for c in range(4):
                idx = hd * 4 + c
                psZ = psP[2] if idx % 2 == 0 else psP[3]
                psAt = psP[0] if idx % 2 == 0 else psP[1]
                wait("te", z_t[c])
                if idx < 2:
                    wait("te", kv_cp[(3, 3)])
                    wait("te", q_sc[63])
                pze = zb_t.get(divmod(idx - 2, 4)) if idx >= 2 else None
                if pze is not None:
                    wait("te", pze)
                ztok = te(lambda e, hd=hd, c=c, psZ=psZ: e.matmul(
                    psZ[:, 0:512], INDH(hd), ZALL[:, c * 512:(c + 1) * 512],
                    start=True, stop=True))
                pae = att.get(divmod(idx - 2, 4)) if idx >= 2 else None
                if pae is not None:
                    wait("te", pae)
                atok = te(lambda e, hd=hd, c=c, psAt=psAt: e.matmul(
                    psAt[:, 0:512], DKB[:, hd * 129:hd * 129 + 128],
                    Yt(hd)[:, c * 512:(c + 1) * 512],
                    start=True, stop=True))
                zdst = BC if idx % 2 == 0 else BC2
                wait("sc", ztok)
                zb_t[(hd, c)] = sc(lambda e, c=c, psZ=psZ, zdst=zdst:
                                   e.activation(zdst[:, c * 512:(c + 1) * 512],
                                                psZ[:, 0:512], AF.Copy))
                wait("ve", zb_t[(hd, c)])
                wait("ve", atok)
                att[(hd, c)] = ve(lambda e, hd=hd, c=c, psAt=psAt, zdst=zdst:
                                  e.tensor_mul(
                                      Xt(hd)[:, c * 512:(c + 1) * 512],
                                      psAt[:, 0:512],
                                      zdst[:, c * 512:(c + 1) * 512]))
        attn_done = att[(H - 1, 3)]
        if "t_attnT" in tap_out:
            store_toks.append(SQ.issue(lambda e: e.dma_start(
                out=tap_out["t_attnT"], in_=X[:, :]), pre_waits=[attn_done]))

        psS = {(0, 0): psP[4][0:1, 0:512], (0, 1): psP[6][0:1, 0:512],
               (1, 0): psP[5][0:1, 0:512], (1, 1): psP[7][0:1, 0:512]}
        bc2_toks = []
        last_sqstat = None
        sqstat_hist = {}
        pdma8 = {0: [], 1: []}
        wo_sc = {}
        wlast = {}
        p8vm = None
        SQTs = [VE2[:, 0:512], VE2[:, 512:1024]]
        for nh in range(2):
            st_toks = {}
            for dc4 in range(4):
                buf = CH[dc4 % 2]
                if nh == 0 and dc4 < 2:
                    pre = [q_last[2 + dc4]]
                elif dc4 < 2:
                    pre = [wlast[(0, 2 + dc4)]]
                else:
                    pre = [wlast[(nh, dc4 - 2)]]
                wch = GQ.issue(lambda e, dc4=dc4, buf=buf: e.dma_start(
                    out=buf.rearrange("p (k f) -> p k f", k=16),
                    in_=woT[:, dc4 * 512:(dc4 + 1) * 512].rearrange(
                        "(k p) f -> p k f", p=128)), pre_waits=pre)
                for dtl in range(4):
                    dt = 4 * dc4 + dtl
                    for cc in range(2):
                        idx = dt * 2 + cc
                        ps = psA if idx % 2 == 0 else psB
                        if dtl == 0 and cc == 0:
                            wait("te", wch)
                        if nh == 0 and idx == 0:
                            wait("te", attn_done)
                        pk = wo_sc.get((nh, idx - 2)) if idx >= 2 else \
                            (wo_sc.get((0, 30 + idx)) if nh == 1 else None)
                        if pk is not None:
                            wait("te", pk)
                        if nh == 1 and idx == 0 and p8vm is not None:
                            wait("te", p8vm)
                        SQT = SQTs[idx % 2]
                        for k in range(NT - 1):
                            te_(lambda e, k=k, buf=buf, dtl=dtl, cc=cc, nh=nh,
                                ps=ps: e.matmul(
                                    ps[:, 0:512],
                                    buf[:, k * 512 + dtl * 128:
                                        k * 512 + (dtl + 1) * 128],
                                    Xt(k)[:, nh * 1024 + cc * 512:
                                          nh * 1024 + (cc + 1) * 512],
                                    start=(k == 0), stop=False))
                        mmt = te(lambda e, buf=buf, dtl=dtl, cc=cc, nh=nh,
                                 ps=ps: e.matmul(
                                     ps[:, 0:512],
                                     buf[:, (NT - 1) * 512 + dtl * 128:
                                         (NT - 1) * 512 + (dtl + 1) * 128],
                                     Xt(NT - 1)[:, nh * 1024 + cc * 512:
                                           nh * 1024 + (cc + 1) * 512],
                                     start=False, stop=True))
                        wlast[(nh, dc4)] = mmt
                        wait("sc", mmt)
                        ev = sc(lambda e, dt=dt, cc=cc, nh=nh, ps=ps:
                                e.activation(
                                    Yt(dt)[:, nh * 1024 + cc * 512:
                                           nh * 1024 + (cc + 1) * 512],
                                    ps[:, 0:512], AF.Identity,
                                    bias=PP[:, 16 + dt:17 + dt]))
                        pq = sqstat_hist.get((nh, idx - 2)) if idx >= 2 else \
                            (sqstat_hist.get((0, 30 + idx)) if nh == 1
                             else None)
                        if pq is not None:
                            wait("sc", pq)
                        sqk = sc(lambda e, dt=dt, cc=cc, nh=nh, SQT=SQT:
                                 e.activation(
                                     SQT,
                                     Yt(dt)[:, nh * 1024 + cc * 512:
                                            nh * 1024 + (cc + 1) * 512],
                                     AF.Square))
                        wo_sc[(nh, idx)] = sqk
                        wait("te", ev)
                        wait("te", sqk)
                        te_(lambda e, cc=cc, dt=dt, nh=nh: e.matmul(
                            psS[(0, cc)], ONESC,
                            Yt(dt)[:, nh * 1024 + cc * 512:
                                   nh * 1024 + (cc + 1) * 512],
                            start=(dt == 0), stop=(dt == NT - 1)))
                        st_toks[cc] = te(lambda e, cc=cc, dt=dt, SQT=SQT:
                                         e.matmul(
                                             psS[(1, cc)], ONESC, SQT,
                                             start=(dt == 0),
                                             stop=(dt == NT - 1)))
                        sqstat_hist[(nh, idx)] = st_toks[cc]
                        last_sqstat = st_toks[cc]
            for cc in range(2):
                gc = nh * 1024 + cc * 512
                gpar = (nh * 2 + cc) % 2
                b0, b1, tm, ts, vmx = ln_rows(psS[(0, cc)], psS[(1, cc)],
                                              gc, st_toks[cc], pdma8[gpar],
                                              rb=(None if gpar == 0
                                                  else OROW))
                pdma8[gpar] = [tm, ts]
                bc2_toks += [b0, b1]
                p8vm = vmx
        if "t_out2T" in tap_out:
            store_toks.append(SQ.issue(lambda e: e.dma_start(
                out=tap_out["t_out2T"], in_=Y[:, :]),
                pre_waits=[wo_sc[(1, 31)]]))

        psS1b = lambda c: psP[c][0:1, 0:512]
        psS2b = lambda c: psP[4 + c][0:1, 0:512]
        p9_ve = {}
        p9_st = {}
        p9_sc = {}
        last_stat9 = None
        for t in range(NT):
            sl = t % 2
            pre = [p9_ve[t - 2]] if t >= 2 else \
                [kv_out, wlast[(1, 3)], wlast[(1, 2)]]
            ld = SQ.issue(lambda e, t=t, sl=sl: e.dma_start(
                out=XT[:, sl * 2048:(sl + 1) * 2048],
                in_=xT[t * 128:(t + 1) * 128, :]), pre_waits=pre)
            wait("gp", ld)
            if t == 0:
                for b in bc2_toks:
                    wait("gp", b)
            xbslot = WB_F[1 + sl][:, 0:2048] if sl == 1 else \
                WB_F[1]
            if t >= 2:
                wait("gp", p9_st[t - 2])
                wait("gp", p9_sc[t - 2])
            tmp = TMPF if t % 2 == 0 else OST.ap()[:, 0:2048]
            if t >= 2:
                wait("gp", p9_ve[t - 2])
            if t == 0:
                wait("gp", wlast[(1, 2)])
                wait("gp", q_done)
            g9 = gp(lambda e, t=t, tmp=tmp: e.tensor_sub(tmp, Yt(t),
                                                         BC2[:, :]))
            wait("ve", g9)
            ve_(lambda e, tmp=tmp: e.tensor_mul(tmp, tmp, BC[:, :]))
            p9_ve[t] = ve(lambda e, sl=sl, xbslot=xbslot, tmp=tmp:
                          e.tensor_add(
                              xbslot, tmp,
                              XT[:, sl * 2048:(sl + 1) * 2048]))
            p9_st[t] = SQ.issue(lambda e, t=t, xbslot=xbslot: e.dma_start(
                out=xb_dram[t * 128:(t + 1) * 128, :], in_=xbslot),
                pre_waits=[p9_ve[t]])
            wait("sc", p9_ve[t])
            p9_sc[t] = sc(lambda e, t=t, xbslot=xbslot: e.activation(
                Xt(t), xbslot, AF.Copy))
            sq9 = sc(lambda e, t=t: e.activation(Yt(t), Xt(t), AF.Square))
            wait("te", p9_sc[t])
            wait("te", sq9)
            for c in range(4):
                te_(lambda e, t=t, c=c: e.matmul(
                    psS1b(c), ONESC, Xt(t)[:, c * 512:(c + 1) * 512],
                    start=(t == 0), stop=(t == NT - 1)))
                last_stat9 = te(lambda e, t=t, c=c: e.matmul(
                    psS2b(c), ONESC, Yt(t)[:, c * 512:(c + 1) * 512],
                    start=(t == 0), stop=(t == NT - 1)))
        if "t_xB" in tap_out:
            store_toks.append(SQ.issue(lambda e: e.dma_start(
                out=tap_out["t_xB"], in_=xb_dram),
                pre_waits=[p9_st[NT - 1]]))
        bc3 = []
        pdma = {0: [], 1: []}
        for c in range(4):
            b0, b1, tm, ts, _ = ln_rows(psS1b(c), psS2b(c), c * 512,
                                        last_stat9, pdma[c % 2],
                                        rb=(None if c % 2 == 0 else OROW))
            pdma[c % 2] = [tm, ts]
            bc3 += [b0, b1]

        x3_toks = [None] * NT
        vm10 = None
        for t in range(NT):
            tmp = TMPF if t % 2 == 0 else OST.ap()[:, 0:2048]
            if t % 2 == 0:
                if t == 0:
                    for b in bc3:
                        wait("gp", b)
                    wait("gp", tok_scales)
                if t >= 2:
                    wait("gp", x3_toks[t - 2])
                g10 = gp(lambda e, t=t, tmp=tmp: e.tensor_sub(tmp, Xt(t),
                                                              BC2[:, :]))
                wait("ve", g10)
            else:
                if t == 1:
                    for b in bc3:
                        wait("ve", b)
                if t >= 2:
                    wait("ve", x3_toks[t - 2])
                ve_(lambda e, t=t, tmp=tmp: e.tensor_sub(tmp, Xt(t),
                                                         BC2[:, :]))
            vm10 = ve(lambda e, tmp=tmp: e.tensor_mul(tmp, tmp, BC[:, :]))
            wait("sc", vm10)
            x3_toks[t] = sc(lambda e, t=t, tmp=tmp: e.activation(
                Yt(t), tmp, AF.Copy, scale=PP[:, 32 + t:33 + t]))
        x3_done = x3_toks[NT - 1]
        if "t_x3sT" in tap_out:
            store_toks.append(SQ.issue(lambda e: e.dma_start(
                out=tap_out["t_x3sT"], in_=Y[:, :]), pre_waits=[x3_done]))

        h128_ev = {}
        for t in range(NT):
            for c in range(4):
                idx = t * 4 + c
                ps = psP[0] if idx % 2 == 0 else psP[1]
                wait("te", x3_toks[t])
                if idx == 0:
                    wait("te", last_stat9)
                pk = h128_ev.get(idx - 2)
                if pk is not None:
                    wait("te", pk)
                mt = te(lambda e, t=t, c=c, ps=ps: e.matmul(
                    ps[:, 0:512], H128, Yt(t)[:, c * 512:(c + 1) * 512],
                    start=True, stop=True))
                wait("sc", mt)
                h128_ev[idx] = sc(lambda e, t=t, c=c, ps=ps: e.activation(
                    Xt(t)[:, c * 512:(c + 1) * 512], ps[:, 0:512], AF.Copy))
        h128_done = h128_ev[NT * 4 - 1]
        prev_level = {a: h128_done for a in range(NT)}
        for lvl in range(4):
            bit = 1 << lvl
            src, dst = (X, Y) if lvl % 2 == 0 else (Y, X)
            cur = {}
            for a in range(NT):
                b_ = a ^ bit
                wait("ve", prev_level[min(a, b_)])
                wait("ve", prev_level[max(a, b_)])
                if a < b_:
                    cur[a] = ve(lambda e, a=a, b_=b_, src=src, dst=dst:
                                e.tensor_add(dst[:, a * 2048:(a + 1) * 2048],
                                             src[:, a * 2048:(a + 1) * 2048],
                                             src[:, b_ * 2048:(b_ + 1) * 2048]))
                else:
                    cur[a] = ve(lambda e, a=a, b_=b_, src=src, dst=dst:
                                e.tensor_sub(dst[:, a * 2048:(a + 1) * 2048],
                                             src[:, b_ * 2048:(b_ + 1) * 2048],
                                             src[:, a * 2048:(a + 1) * 2048]))
            prev_level = cur
        fT_done = prev_level[NT - 1]
        if "t_fT" in tap_out:
            store_toks.append(SQ.issue(lambda e: e.dma_start(
                out=tap_out["t_fT"], in_=X[:, :]), pre_waits=[fT_done]))

        wdl = []
        for g in range(4):
            wdl.append(GQ.issue(lambda e, g=g: e.dma_start(
                out=Y[:, 16384 + g * 4096:16384 + (g + 1) * 4096].rearrange(
                    "p (k f) -> p k f", k=16),
                in_=wd_slice[g * 2048:(g + 1) * 2048, :].rearrange(
                    "(k p) f -> p k f", p=128)), pre_waits=[fT_done]))
        YA = lambda k: Y[:, k * 256:(k + 1) * 256]
        YB = lambda k: Y[:, 16384 + k * 256:16384 + (k + 1) * 256]
        wde = {}
        for k in range(ITI):
            ps = psP[2] if k % 2 == 0 else psP[3]
            if k % 16 == 0:
                wait("te", wdl[k // 16])
            if k == 0:
                wait("te", attn_done)
                wait("te", zb_t[(H - 1, 3)])
            pk = wde.get(k - 2)
            if pk is not None:
                wait("te", pk)
            mt = te(lambda e, k=k, ps=ps: e.matmul(
                ps[:, 0:256], H128, YB(k), start=True, stop=True))
            wait("sc", mt)
            wde[k] = sc(lambda e, k=k, ps=ps: e.activation(
                YA(k), ps[:, 0:256], AF.Copy))
        prevw = {a: wde[a] for a in range(ITI)}
        for lvl in range(6):
            bit = 1 << lvl
            srcf, dstf = (YA, YB) if lvl % 2 == 0 else (YB, YA)
            cur = {}
            for a in range(ITI):
                b_ = a ^ bit
                wait("ve", prevw[min(a, b_)])
                wait("ve", prevw[max(a, b_)])
                if a < b_:
                    cur[a] = ve(lambda e, a=a, b_=b_, srcf=srcf, dstf=dstf:
                                e.tensor_add(dstf(a), srcf(a), srcf(b_)))
                else:
                    cur[a] = ve(lambda e, a=a, b_=b_, srcf=srcf, dstf=dstf:
                                e.tensor_sub(dstf(a), srcf(b_), srcf(a)))
            prevw = cur
        wcomb_done = prevw[ITI - 1]
        contrib_toks = []
        for g in range(4):
            for dtl in range(2):
                contrib_toks.append(SQ.issue(lambda e, g=g, dtl=dtl:
                                             e.dma_start(
                    out=wd_contrib.ap().rearrange(
                        "(dtl k p) f -> p dtl k f", dtl=2, k=64)[
                        :, dtl, g * 16:(g + 1) * 16, :],
                    in_=Y[:, g * 4096:(g + 1) * 4096]
                    .rearrange("p (k g2 f) -> p g2 k f", k=16, g2=2)[
                        :, dtl, :, :]),
                    pre_waits=[wcomb_done]))
        ag_tok = None

        hst = [BC, BC2]
        hstore = {}
        hmul = {}
        uge_t = {}
        uge_half = {}
        ulast = {}
        p9_wb1 = p9_st[14]
        p9_wb2 = p9_st[15]
        for ic in range(32):
            bu = ic % 2
            if ic == 0:
                pre_u = [x3_done]
            elif ic == 1:
                pre_u = [p9_wb1]
            else:
                pre_u = [ulast[ic - 2]]
            utok = GQ.issue(lambda e, ic=ic, bu=bu: e.dma_start(
                out=WB[bu].rearrange("p (k f) -> p k f", k=16),
                in_=wupT[:, ic * 256:(ic + 1) * 256].rearrange(
                    "(k p) f -> p k f", p=128)), pre_waits=pre_u)
            guse_wb2 = (ic % 2 == 0)
            if ic == 0:
                pre_g = [p9_wb2]
            elif ic == 1:
                pre_g = [x3_done]
            else:
                pre_g = [ulast[ic - 2]]
            if guse_wb2:
                gtok = GQ.issue(lambda e, ic=ic: e.dma_start(
                    out=WB[2].rearrange("p (k f) -> p k f", k=16),
                    in_=wgT[:, ic * 256:(ic + 1) * 256].rearrange(
                        "(k p) f -> p k f", p=128)), pre_waits=pre_g)
            else:
                gtok = GQ.issue(lambda e, ic=ic: e.dma_start(
                    out=UGEHI.rearrange("p (k f) -> p k f", k=16),
                    in_=wgT[:, ic * 256:(ic + 1) * 256].rearrange(
                        "(k p) f -> p k f", p=128)), pre_waits=pre_g)
            if ic == 4:
                for ct in contrib_toks:
                    wait("gp", ct)
                agv = AGS.nxt()
                em("gp", lambda e: e.collective_compute(
                    "AllGather", ALU.bypass,
                    replica_groups=[list(range(NCORES))],
                    ins=[wd_contrib.ap().opt()], outs=[wdc.ap().opt()]
                ).then_inc(AGS.h, 1))
                ag_tok = (AGS.h, agv)
            for itl in range(2):
                it = 2 * ic + itl
                hs = hst[it % 2]
                if itl == 0:
                    wait("te", utok)
                    wait("te", gtok)
                for hv in (0, 1):
                    cs = (2 * hv, 2 * hv + 1)
                    if it == 0 and hv == 0:
                        wait("te", fT_done)
                    if it == 0:
                        wait("te", wde[ITI - 1])
                    pu = uge_half.get((it - 1, hv))
                    if pu is not None:
                        wait("te", pu)
                    mmug = None
                    for k in range(NT):
                        for c in cs:
                            te_(lambda e, k=k, c=c, bu=bu, itl=itl: e.matmul(
                                psP[c][:, 0:512],
                                WB[bu][:, k * 256 + itl * 128:
                                      k * 256 + (itl + 1) * 128],
                                Xt(k)[:, c * 512:(c + 1) * 512],
                                start=(k == 0), stop=(k == NT - 1)))
                        for c in cs:
                            if guse_wb2:
                                lhs = WB[2][:, k * 256 + itl * 128:
                                            k * 256 + (itl + 1) * 128]
                            else:
                                lhs = UGEHI[:, k * 256 + itl * 128:
                                            k * 256 + (itl + 1) * 128]
                            if k == NT - 1 and c == cs[1]:
                                mmug = te(lambda e, lhs=lhs, c=c: e.matmul(
                                    psP[4 + c][:, 0:512], lhs,
                                    Xt(NT - 1)[:, c * 512:(c + 1) * 512],
                                    start=False, stop=True))
                            else:
                                te_(lambda e, k=k, c=c, lhs=lhs: e.matmul(
                                    psP[4 + c][:, 0:512], lhs,
                                    Xt(k)[:, c * 512:(c + 1) * 512],
                                    start=(k == 0), stop=(k == NT - 1)))
                    if hv == 1:
                        ulast[ic] = mmug
                    wait("sc", mmug)
                    pvm = hmul.get((it - 1, cs[1]))
                    if pvm is not None:
                        wait("sc", pvm)
                    ug2 = None
                    for c in cs:
                        sc_(lambda e, it=it, c=c: e.activation(
                            UGE[:, c * 512:(c + 1) * 512], psP[c][:, 0:512],
                            AF.Copy, scale=PP[:, 64 + it:65 + it]))
                        ug2 = sc(lambda e, it=it, c=c: e.activation(
                            UGE[:, 2048 + c * 512:2048 + (c + 1) * 512],
                            psP[4 + c][:, 0:512],
                            AF.Silu, scale=PP[:, 128 + it:129 + it]))
                    uge_half[(it, hv)] = ug2
                    for c in cs:
                        wait("ve", uge_half[(it, hv)])
                        if c == cs[0]:
                            pst = hstore.get(it - 2)
                            if pst is not None:
                                wait("ve", pst)
                        hmul[(it, c)] = ve(lambda e, c=c, hs=hs: e.tensor_mul(
                            hs[:, c * 512:(c + 1) * 512],
                            UGE[:, c * 512:(c + 1) * 512],
                            UGE[:, 2048 + c * 512:2048 + (c + 1) * 512]))
                uge_t[it] = uge_half[(it, 1)]
                hstore[it] = SQ.issue(lambda e, it=it, hs=hs: e.dma_start(
                    out=h_dram[it * 128:(it + 1) * 128, :], in_=hs[:, :]),
                    pre_waits=[hmul[(it, 3)]])
        h_done = hstore[63]
        up_evac_last = uge_t[63]
        if "t_h" in tap_out:
            store_toks.append(SQ.issue(lambda e: e.dma_start(
                out=tap_out["t_h"], in_=h_dram), pre_waits=[h_done]))
        if "t_wdc" in tap_out:
            store_toks.append(SQ.issue(lambda e: e.dma_start(
                out=tap_out["t_wdc"], in_=wdc.ap()[0:16384, :]),
                pre_waits=[ag_tok]))

        psq1 = psP[4][0:1, 0:512]
        psq2 = psP[5][0:1, 0:512]
        snap = [(s_.h, s_.c) for s_ in SQ.sems + GQ.sems if s_.c > 0]
        out_toks = []
        dn_ev = {}
        wlast12 = {}
        ln4vm = {}
        fin_ve = {}
        last_qstat = None
        pd12 = []
        YR = lambda dt, kk: Y[:, (dt % 4) * 8192 + kk * 128:
                              (dt % 4) * 8192 + (kk + 1) * 128]
        xbt_toks = {}

        def emit_xbt(q, t):
            prex = [p9_st[t]]
            if q == 0:
                prex.append(up_evac_last)
                if t >= 8:
                    prex.append(fin_ve[(0, t - 8)])
            else:
                if t >= 8:
                    prex.append(fin_ve[(q, t - 8)])
                else:
                    prex.append(fin_ve[(q - 1, 8 + t)])
            xbt_toks[(q, t)] = SQ.issue(lambda e, q=q, t=t: e.dma_start(
                out=WBLF[:, (t % 8) * 512:(t % 8 + 1) * 512],
                in_=xb_dram[t * 128:(t + 1) * 128,
                            q * 512:(q + 1) * 512]), pre_waits=prex)

        def emit_finals(q, b0, b1):
            for t in range(NT):
                osl = (t % 2) * 512
                if (q, t) not in xbt_toks:
                    emit_xbt(q, t)
                xbt = xbt_toks[(q, t)]
                wait("ve", b0)
                wait("ve", b1)
                wait("ve", xbt)
                wait("ve", dn_ev[(q, t)])
                if len(out_toks) >= 2:
                    wait("ve", out_toks[-2])
                ve_(lambda e, t=t, osl=osl, q=q: e.tensor_sub(
                    OST[:, osl:osl + 512], UGE[:, t * 512:(t + 1) * 512],
                    BC2[:, q * 512:(q + 1) * 512]))
                ve_(lambda e, osl=osl, q=q: e.tensor_mul(
                    OST[:, osl:osl + 512], OST[:, osl:osl + 512],
                    BC[:, q * 512:(q + 1) * 512]))
                fin_ve[(q, t)] = ve(lambda e, t=t, osl=osl: e.tensor_add(
                    OST[:, osl:osl + 512], OST[:, osl:osl + 512],
                    WBLF[:, (t % 8) * 512:(t % 8 + 1) * 512]))
                out_toks.append(SQ.issue(
                    lambda e, q=q, t=t, osl=osl: e.dma_start(
                        out=yT[t * 128:(t + 1) * 128, q * 512:(q + 1) * 512],
                        in_=OST[:, osl:osl + 512]),
                    pre_waits=[fin_ve[(q, t)]]))

        pending_finals = None
        ring_toks = {}
        dt15_prog = {}

        def emit_ring(q, dt):
            if q == 0 and dt < 4:
                pre0 = [ag_tok, wcomb_done, fT_done]
                if tap_out:
                    pre0 += list(snap)
            elif dt >= 4:
                pre0 = [wlast12[(q, dt - 4)]]
            else:
                pre0 = [wlast12[(q - 1, 12 + dt)]]
            t0 = GQ.issue(lambda e, dt=dt: e.dma_start(
                out=Y[:, (dt % 4) * 8192:(dt % 4) * 8192 + 4096]
                .rearrange("p (k f) -> p k f", k=32),
                in_=wdc.ap()[dt * 8192:dt * 8192 + 4096, :].rearrange(
                    "(k p) f -> p k f", p=128)), pre_waits=pre0)
            t1 = SQ.issue(lambda e, dt=dt: e.dma_start(
                out=Y[:, (dt % 4) * 8192 + 4096:(dt % 4) * 8192 + 8192]
                .rearrange("p (k f) -> p k f", k=32),
                in_=wdc.ap()[dt * 8192 + 4096:(dt + 1) * 8192, :]
                .rearrange("(k p) f -> p k f", p=128)),
                pre_waits=list(pre0))
            ring_toks[(q, dt)] = (t0, t1)

        for q in range(4):
            for dt in range(4):
                emit_ring(q, dt)
            hq_toks = []
            for g in range(8):
                if q == 0:
                    pre = [up_evac_last] + [hstore[g * 8 + j]
                                            for j in range(8)]
                    if tap_out:
                        pre += list(snap)
                else:
                    pre = [dt15_prog[(q - 1, g)]]
                QQ = GQ
                hq_toks.append(QQ.issue(lambda e, q=q, g=g: e.dma_start(
                    out=X.ap().rearrange("p (i f) -> p i f", i=ITI)[
                        :, g * 8:(g + 1) * 8, :],
                    in_=h_dram.rearrange("(i p) f -> p i f", p=128)[
                        :, g * 8:(g + 1) * 8, q * 512:(q + 1) * 512]),
                    pre_waits=pre))
            if pending_finals is not None:
                emit_finals(*pending_finals)
                pending_finals = None
            if q == 3:
                for t in range(8):
                    emit_xbt(3, t)
            q_stat = None
            for dt in range(16):
                ps = psP[0] if dt % 2 == 0 else psP[1]
                t0, t1 = ring_toks[(q, dt)]
                if dt == 0 and q == 0:
                    wait("te", up_evac_last)
                pde = dn_ev.get((q, dt - 2)) if dt >= 2 else \
                    (dn_ev.get((q - 1, 14 + dt)) if q > 0 else None)
                if pde is not None:
                    wait("te", pde)
                wait("te", t0)
                mmd = None
                for k in range(ITI):
                    if k == 32:
                        wait("te", t1)
                    if dt == 0 and k % 8 == 0:
                        wait("te", hq_toks[k // 8])
                    last = (k == ITI - 1)
                    tokd = (dt == 15 and k % 8 == 7) or last
                    if tokd:
                        mmd = te(lambda e, k=k, dt=dt, ps=ps, last=last:
                                 e.matmul(ps[:, 0:512], YR(dt, k), Xq(k),
                                          start=(k == 0), stop=last))
                        if dt == 15:
                            dt15_prog[(q, k // 8)] = mmd
                    else:
                        te_(lambda e, k=k, dt=dt, ps=ps: e.matmul(
                            ps[:, 0:512], YR(dt, k), Xq(k),
                            start=(k == 0), stop=False))
                wlast12[(q, dt)] = mmd
                if dt + 4 < 16:
                    emit_ring(q, dt + 4)
                wait("sc", mmd)
                if q == 0 and dt == 0:
                    wait("sc", hmul[(63, 3)])
                pfin = fin_ve.get((q - 1, dt)) if q > 0 else None
                if pfin is not None:
                    wait("sc", pfin)
                ev = sc(lambda e, dt=dt, ps=ps: e.activation(
                    UGE[:, dt * 512:(dt + 1) * 512], ps[:, 0:512],
                    AF.Copy, scale=PP[:, 192 + dt:193 + dt]))
                dn_ev[(q, dt)] = ev
                if last_qstat is not None:
                    wait("sc", last_qstat)
                sq12 = sc(lambda e, dt=dt: e.activation(
                    VE2[:, 0:512], UGE[:, dt * 512:(dt + 1) * 512],
                    AF.Square))
                wait("te", ev)
                wait("te", sq12)
                if dt == 0 and q > 0:
                    wait("te", ln4vm[q - 1])
                te_(lambda e, dt=dt: e.matmul(
                    psq1, ONESC, UGE[:, dt * 512:(dt + 1) * 512],
                    start=(dt == 0), stop=(dt == NT - 1)))
                q_stat = te(lambda e, dt=dt: e.matmul(
                    psq2, ONESC, VE2[:, 0:512],
                    start=(dt == 0), stop=(dt == NT - 1)))
                last_qstat = q_stat
            if q == 0 and "t_x5" in tap_out:
                store_toks.append(SQ.issue(lambda e: e.dma_start(
                    out=tap_out["t_x5"], in_=UGE[:, 0:NT * 512]),
                    pre_waits=[dn_ev[(0, 15)]]))
            pd4 = ([h_done] if q == 0 else []) + pd12
            b0, b1, tm4, ts4, vm4 = ln_rows(psq1, psq2, q * 512, q_stat, pd4)
            pd12 = [tm4, ts4]
            ln4vm[q] = vm4
            pending_finals = (q, b0, b1)
        emit_finals(*pending_finals)

        for s_ in SQ.sems + GQ.sems:
            if s_.c > 0:
                wait("sync", (s_.h, s_.c))

        with nc.Block() as block:
            @block.sync
            def _(eng):
                for fn in P.streams["sync"]:
                    fn(eng)

            @block.gpsimd
            def _(eng):
                for fn in P.streams["gp"]:
                    fn(eng)

            @block.scalar
            def _(eng):
                for fn in P.streams["sc"]:
                    fn(eng)

            @block.vector
            def _(eng):
                for fn in P.streams["ve"]:
                    fn(eng)

            @block.tensor
            def _(eng):
                for fn in P.streams["te"]:
                    fn(eng)

    return nc


def _host_inputs(inputs):
    x = np.asarray(inputs["L0_attn_x"])
    Wq = np.asarray(inputs["L0_attn_Wq"]); bq = np.asarray(inputs["L0_attn_bq"])
    Wk = np.asarray(inputs["L0_attn_Wk"]); bk = np.asarray(inputs["L0_attn_bk"])
    Wv = np.asarray(inputs["L0_attn_Wv"]); bv = np.asarray(inputs["L0_attn_bv"])
    Wo = np.asarray(inputs["L0_attn_Wo"]); bo = np.asarray(inputs["L0_attn_bo"])
    su = np.asarray(inputs["L0_ffn_SU_ug"])
    sd = np.asarray(inputs["L0_ffn_SU_dp"])
    wsug = np.asarray(inputs["L0_ffn_Wscale_ug"])
    wsdp = np.asarray(inputs["L0_ffn_Wscale_dp"])
    Wup = np.asarray(inputs["L0_ffn_W_up"])
    Wg = np.asarray(inputs["L0_ffn_W_gate"])
    Wd = np.asarray(inputs["L0_ffn_W_down"])

    f = np.float32
    wqTv = np.ascontiguousarray(Wq.T, dtype=f)
    wkTav = np.ascontiguousarray(
        np.concatenate([Wk.T, bk[None, :]], axis=0), dtype=f)
    wvTav = np.ascontiguousarray(
        np.concatenate([Wv.T, bv[None, :]], axis=0), dtype=f)
    woTv = np.ascontiguousarray(Wo.T, dtype=f)
    wupTv = np.ascontiguousarray(Wup.T, dtype=f)
    wgTv = np.ascontiguousarray(Wg.T, dtype=f)
    wdTv = np.ascontiguousarray(Wd.T, dtype=f)

    pp = np.zeros((128, 208), dtype=f)
    pp[:, 0:16] = bq.reshape(16, 128).T
    pp[:, 16:32] = bo.reshape(16, 128).T
    pp[:, 32:48] = su.reshape(16, 128).T
    pp[:, 64:128] = wsug[:INTER].reshape(64, 128).T
    pp[:, 128:192] = wsug[INTER:].reshape(64, 128).T
    pp[:, 192:208] = wsdp.reshape(16, 128).T
    sd_pp = np.ascontiguousarray(sd.reshape(64, 128).T, dtype=f)

    bidx = np.arange(128, dtype=np.uint32)
    par = np.bitwise_count(bidx[:, None] & bidx[None, :]) % 2
    h128 = np.where(par == 0, 1.0, -1.0).astype(ml_dtypes.bfloat16)
    consts = np.zeros((128, 2308), dtype=ml_dtypes.bfloat16)
    consts[:, 0:128] = h128
    consts[:, 128] = 1.0
    consts[0, 129:257] = 1.0
    for hh in range(16):
        consts[hh, 260 + hh * 128:260 + (hh + 1) * 128] = 1.0

    pos = np.arange(16384, dtype=np.float64)
    decay_full = np.exp(-pos / 16384.0)

    in_maps = []
    for c in range(NCORES):
        xTc = np.ascontiguousarray(x[0, c * N:(c + 1) * N, :].T, dtype=f)
        dslice = decay_full[c * N:(c + 1) * N]
        ppc = pp.copy()
        ppc[:, 48:64] = dslice.reshape(16, 128).T.astype(f)
        decay_bc = np.ascontiguousarray(np.broadcast_to(
            dslice.astype(ml_dtypes.bfloat16)[None, :], (128, N)))
        wd_sl = np.ascontiguousarray(wdTv[:, c * 256:(c + 1) * 256])
        in_maps.append({
            "xT": xTc, "wqT": wqTv, "wkTa": wkTav, "wvTa": wvTav, "woT": woTv,
            "wupT": wupTv, "wgT": wgTv, "wd_slice": wd_sl,
            "pp_in": ppc, "sd_in": sd_pp, "consts_in": consts,
            "decay_in": decay_bc,
        })
    return in_maps


def kernel(**inputs):
    in_maps = _host_inputs(inputs)
    nc = build_nc()
    res = run_bass_kernel_spmd(nc, in_maps, core_ids=list(range(NCORES)))
    out = np.empty((1, 16384, D), dtype=np.float32)
    for c in range(NCORES):
        out[0, c * N:(c + 1) * N, :] = np.asarray(res.results[c]["yT"]).T
    return out


# version 16 (final)
# speedup vs baseline: 1.0350x; 1.0350x over previous
import math
from contextlib import ExitStack

import numpy as np
import ml_dtypes

import concourse.bass as bass
import concourse.mybir as mybir
from concourse.bass_utils import run_bass_kernel_spmd

F32 = mybir.dt.float32
BF16 = mybir.dt.bfloat16
AF = mybir.ActivationFunctionType
ALU = mybir.AluOpType

D = 2048
N = 2048
NT = 16
H = 16
INTER = 8192
ITI = 64
NCORES = 8
EPS_LN = 1e-5
EPS_ATTN = 1e-6
RS2048 = 1.0 / math.sqrt(2048.0)
RS8192 = 1.0 / math.sqrt(8192.0)

ENGS = ("sync", "gp", "sc", "ve", "te")


class Sem:
    def __init__(self, h):
        self.h = h
        self.c = 0

    def nxt(self, n=1):
        self.c += n
        return self.c


class Prog:
    def __init__(self):
        self.streams = {k: [] for k in ENGS}

    def em(self, eng, fn):
        self.streams[eng].append(fn)

    def wait(self, eng, tok):
        if tok is None:
            return
        h, v = tok
        self.em(eng, lambda e, h=h, v=v: e.wait_ge(h, v))


class DmaQ:

    def __init__(self, prog, eng, sems):
        self.p = prog
        self.eng = eng
        self.sems = sems
        self.i = 0

    def issue(self, fn, pre_waits=()):
        s = self.sems[self.i % len(self.sems)]
        self.i += 1
        if s.c > 0:
            self.p.wait(self.eng, (s.h, s.c))
        for t in pre_waits:
            self.p.wait(self.eng, t)
        v = s.nxt(16)
        self.p.em(self.eng, lambda e, fn=fn, h=s.h: fn(e).then_inc(h, 16))
        return (s.h, v)


def build_nc(taps=()):
    nc = bass.Bass("TRN2", target_bir_lowering=False)

    xT = nc.dram_tensor("xT", [D, N], F32, kind="ExternalInput").ap()
    wqT = nc.dram_tensor("wqT", [D, D], F32, kind="ExternalInput").ap()
    wkTa = nc.dram_tensor("wkTa", [D + 1, D], F32, kind="ExternalInput").ap()
    wvTa = nc.dram_tensor("wvTa", [D + 1, D], F32, kind="ExternalInput").ap()
    woT = nc.dram_tensor("woT", [D, D], F32, kind="ExternalInput").ap()
    wupT = nc.dram_tensor("wupT", [D, INTER], F32, kind="ExternalInput").ap()
    wgT = nc.dram_tensor("wgT", [D, INTER], F32, kind="ExternalInput").ap()
    wd_slice = nc.dram_tensor("wd_slice", [INTER, 256], F32,
                              kind="ExternalInput").ap()
    pp_in = nc.dram_tensor("pp_in", [128, 208], F32, kind="ExternalInput").ap()
    sd_in = nc.dram_tensor("sd_in", [128, 64], F32, kind="ExternalInput").ap()
    consts_in = nc.dram_tensor("consts_in", [128, 2308], BF16,
                               kind="ExternalInput").ap()
    decay_in = nc.dram_tensor("decay_in", [128, N], BF16,
                              kind="ExternalInput").ap()

    yT = nc.dram_tensor("yT", [D, N], F32, kind="ExternalOutput").ap()

    wdc = nc.dram_tensor("wdc", [16 * 64 * 128, 128], BF16, kind="Internal")
    wd_contrib = nc.dram_tensor("wd_contrib", [2 * 64 * 128, 128], BF16,
                                kind="Internal")
    h_dram = nc.dram_tensor("h_dram", [INTER, N], BF16, kind="Internal").ap()
    xb_dram = nc.dram_tensor("xb_dram", [D, N], F32, kind="Internal").ap()
    bcd = nc.dram_tensor("bcd", [4, N], F32, kind="Internal").ap()
    cc_in = nc.dram_tensor("cc_kv_in", [128, H * 129], F32, kind="Internal")
    cc_out = nc.dram_tensor("cc_kv_out", [128, H * 129], F32, kind="Internal")

    tap_out = {}
    for tname, shape, dt in (
        ("t_x1T", [128, NT * 2048], BF16), ("t_K", [128, NT * 2048], BF16),
        ("t_KV", [128, H * 129], F32), ("t_QT", [128, NT * 2048], BF16),
        ("t_attnT", [128, NT * 2048], BF16),
        ("t_out2T", [128, NT * 2048], BF16),
        ("t_xB", [D, N], F32), ("t_x3sT", [128, NT * 2048], BF16),
        ("t_fT", [128, NT * 2048], BF16), ("t_h", [INTER, N], BF16),
        ("t_x5", [128, NT * 512], BF16), ("t_wdc", [16384, 128], BF16),
    ):
        if tname in taps:
            tap_out[tname] = nc.dram_tensor(tname, shape, dt,
                                            kind="ExternalOutput").ap()

    P = Prog()
    em, wait = P.em, P.wait

    with ExitStack() as ctx:
        sb = lambda nm, shape, dt: ctx.enter_context(nc.sbuf_tensor(nm, shape, dt))
        X = sb("Xb", [128, NT * 2048], BF16)
        Y = sb("Yb", [128, NT * 2048], BF16)
        XT = sb("XTb", [128, 4096], F32)
        WBL = sb("WBL", [128, 8192], BF16)
        WB = [WBL.ap()[:, 0:4096], WBL.ap()[:, 4096:8192],
              sb("WB2t", [128, 4096], BF16)]
        BC = sb("BCs", [128, 2048], BF16)
        BC2 = sb("BCm", [128, 2048], BF16)
        DKB = sb("DKB", [128, 2080], BF16)
        ROWS = sb("ROWS", [1, 1024], F32)
        ZALL = sb("ZALL", [16, 2048], BF16)
        KSB = sb("KSB", [128, 256], BF16)
        BROW = sb("BROW", [1, 1024], BF16)
        VE2 = sb("VE2", [128, 1040], BF16)
        CONSTS = sb("CONSTS", [128, 2308], BF16)
        PP = sb("PP", [128, 208], F32)
        OST = sb("OST", [128, 2080], F32)
        psP = [ctx.enter_context(nc.psum_tensor(f"psP{i}", [128, 512], F32))
               for i in range(8)]

        H128 = CONSTS[:, 0:128]
        ONESC = CONSTS[:, 128:129]
        ONESR = CONSTS[0:1, 129:257]
        INDH = lambda h: CONSTS[0:16, 260 + h * 128:260 + (h + 1) * 128]

        WBLF = WBL.ap().bitcast(F32)
        WB_F = [WBLF[:, 0:2048], WBLF[:, 2048:4096],
                WB[2].ap().bitcast(F32)]
        TMPF = WB_F[0]
        UGE = XT.ap().bitcast(BF16)
        UGEHI = UGE[:, 4096:8192]

        mksem = lambda nm: Sem(ctx.enter_context(nc.semaphore(nm)))
        SQ = DmaQ(P, "sync", [mksem(f"sq{i}") for i in range(10)])
        GQ = DmaQ(P, "gp", [mksem(f"gq{i}") for i in range(10)])
        TE, VE, SC, GP, CCS, AGS = (mksem(n) for n in
                                    ("teP", "veP", "scP", "gpP", "ccS", "agS"))

        def mk(sem, eng):
            def f(fn):
                v = sem.nxt()
                em(eng, lambda e, fn=fn: fn(e).then_inc(sem.h, 1))
                return (sem.h, v)
            return f
        te, ve, sc, gp = mk(TE, "te"), mk(VE, "ve"), mk(SC, "sc"), mk(GP, "gp")
        te_ = lambda fn: em("te", fn)
        ve_ = lambda fn: em("ve", fn)
        sc_ = lambda fn: em("sc", fn)

        Xt = lambda t: X[:, t * 2048:(t + 1) * 2048]
        Yt = lambda t: Y[:, t * 2048:(t + 1) * 2048]
        Xq = lambda i: X[:, i * 512:(i + 1) * 512]

        store_toks = []

        OROW = OST.ap()[0:1, 0:1024]

        def ln_rows(ps1_ap, ps2_ap, chunk_slot, stat_tok, prev_dma,
                    denom=2048.0, rb=None):
            lo = chunk_slot
            R = ROWS.ap() if rb is None else rb
            wait("ve", stat_tok)
            for t in prev_dma:
                wait("ve", t)
            ve_(lambda e: e.tensor_scalar_mul(R[0:1, 0:512], ps1_ap,
                                              1.0 / denom))
            vm = ve(lambda e: e.tensor_scalar_mul(R[0:1, 512:1024], ps2_ap,
                                                  1.0 / denom))
            tm = SQ.issue(lambda e: e.dma_start(
                out=bcd[1:2, lo:lo + 512], in_=R[0:1, 0:512]),
                pre_waits=[vm])
            wait("sc", tm)
            sq_t = sc(lambda e: e.activation(R[0:1, 0:512], R[0:1, 0:512],
                                             AF.Square))
            wait("ve", sq_t)
            ve_(lambda e: e.tensor_sub(R[0:1, 512:1024], R[0:1, 512:1024],
                                       R[0:1, 0:512]))
            va = ve(lambda e: e.tensor_scalar_add(R[0:1, 512:1024],
                                                  R[0:1, 512:1024], EPS_LN))
            wait("sc", va)
            sc_(lambda e: e.activation(R[0:1, 512:1024], R[0:1, 512:1024],
                                       AF.Ln))
            si = sc(lambda e: e.activation(R[0:1, 512:1024],
                                           R[0:1, 512:1024], AF.Exp,
                                           scale=-0.5))
            ts = SQ.issue(lambda e: e.dma_start(
                out=bcd[0:1, lo:lo + 512], in_=R[0:1, 512:1024]),
                pre_waits=[si])
            b0 = GQ.issue(lambda e: e.dma_start(
                out=BC[:, lo:lo + 512],
                in_=bcd[0:1, lo:lo + 512].partition_broadcast(128)),
                pre_waits=[ts])
            b1 = GQ.issue(lambda e: e.dma_start(
                out=BC2[:, lo:lo + 512],
                in_=bcd[1:2, lo:lo + 512].partition_broadcast(128)),
                pre_waits=[tm])
            return b0, b1, tm, ts, vm

        tok_pp = SQ.issue(lambda e: e.dma_start(out=PP[:, :], in_=pp_in))
        tok_sd = SQ.issue(lambda e: e.dma_start(
            out=VE2.ap().bitcast(F32)[:, 0:64], in_=sd_in))
        tok_cst = SQ.issue(lambda e: e.dma_start(out=CONSTS[:, :], in_=consts_in))
        tok_dk = SQ.issue(lambda e: e.dma_start(out=DKB[:, 0:2048], in_=decay_in))
        wait("ve", tok_pp)
        wait("ve", tok_sd)
        ve_(lambda e: e.tensor_mul(PP[:, 64:128], PP[:, 64:128],
                                   VE2.ap().bitcast(F32)[:, 0:64]))
        ve_(lambda e: e.tensor_scalar_mul(PP[:, 64:128], PP[:, 64:128], RS2048))
        ve_(lambda e: e.tensor_scalar_mul(PP[:, 128:192], PP[:, 128:192], RS2048))
        tok_scales = ve(lambda e: e.tensor_scalar_mul(PP[:, 192:208],
                                                      PP[:, 192:208], RS8192))
        tok_ones = None
        for s in (0, 1):
            vslot = VE2[:, s * 520:s * 520 + 516].rearrange(
                "p (j f) -> p j f", j=4)
            tok_ones = ve(lambda e, vslot=vslot: e.memset(
                vslot[:, :, 128:129], 1.0))

        psS1 = lambda c: psP[c][0:1, 0:512]
        psS2 = lambda c: psP[4 + c][0:1, 0:512]
        last_stat = None
        for t in range(NT):
            ld = GQ.issue(lambda e, t=t: e.dma_start(
                out=Xt(t), in_=xT[t * 128:(t + 1) * 128, :]))
            wait("sc", ld)
            sq_tok = sc(lambda e, t=t: e.activation(Yt(t), Xt(t), AF.Square))
            if t == 0:
                wait("te", tok_cst)
            wait("te", ld)
            wait("te", sq_tok)
            for c in range(4):
                te_(lambda e, t=t, c=c: e.matmul(
                    psS1(c), ONESC, Xt(t)[:, c * 512:(c + 1) * 512],
                    start=(t == 0), stop=(t == NT - 1)))
                last_stat = te(lambda e, t=t, c=c: e.matmul(
                    psS2(c), ONESC, Yt(t)[:, c * 512:(c + 1) * 512],
                    start=(t == 0), stop=(t == NT - 1)))
        bc1 = []
        pdma = {0: [], 1: []}
        for c in range(4):
            b0, b1, tm, ts, _ = ln_rows(psS1(c), psS2(c), c * 512,
                                        last_stat, pdma[c % 2],
                                        rb=(None if c % 2 == 0 else OROW))
            pdma[c % 2] = [tm, ts]
            bc1 += [b0, b1]

        x1_toks = [None] * NT
        for t in range(NT):
            sl = t % 2
            pre = [x1_toks[t - 2]] if t >= 2 else []
            ld = (SQ if sl == 0 else GQ).issue(
                lambda e, t=t, sl=sl: e.dma_start(
                    out=XT[:, sl * 2048:(sl + 1) * 2048],
                    in_=xT[t * 128:(t + 1) * 128, :]), pre_waits=pre)
            tmp = TMPF if t % 2 == 0 else OST.ap()[:, 0:2048]
            if sl == 0:
                wait("gp", ld)
                if t == 0:
                    for b in bc1:
                        wait("gp", b)
                if t >= 2:
                    wait("gp", x1_toks[t - 2])
                gsub = gp(lambda e, sl=sl, tmp=tmp: e.tensor_sub(
                    tmp, XT[:, sl * 2048:(sl + 1) * 2048], BC2[:, :]))
                wait("ve", gsub)
            else:
                wait("ve", ld)
                if t == 1:
                    for b in bc1:
                        wait("ve", b)
                ve_(lambda e, sl=sl, tmp=tmp: e.tensor_sub(
                    tmp, XT[:, sl * 2048:(sl + 1) * 2048], BC2[:, :]))
            x1_toks[t] = ve(lambda e, t=t, tmp=tmp: e.tensor_mul(
                Xt(t), tmp, BC[:, :]))
        x1_done = x1_toks[NT - 1]
        if "t_x1T" in tap_out:
            store_toks.append(SQ.issue(lambda e: e.dma_start(
                out=tap_out["t_x1T"], in_=X[:, :]), pre_waits=[x1_done]))

        psA, psB = psP[0], psP[1]
        CH = [WBL.ap(), UGE[:, 0:8192]]

        def rm_phase512(wsrc, evac_fn, init_guards, kv_hook=None):
            ev = {}
            last_mm = {}
            for dc4 in range(4):
                buf = CH[dc4 % 2]
                bs = (dc4 % 2) * 512
                pre = list(init_guards[dc4 % 2]) if dc4 < 2 else \
                    [last_mm[dc4 - 2]]
                wtok = GQ.issue(lambda e, dc4=dc4, buf=buf: e.dma_start(
                    out=buf.rearrange("p (k f) -> p k f", k=16),
                    in_=wsrc[0:2048, dc4 * 512:(dc4 + 1) * 512].rearrange(
                        "(k p) f -> p k f", p=128)), pre_waits=pre)
                btok = GQ.issue(lambda e, dc4=dc4, bs=bs: e.dma_start(
                    out=BROW[0:1, bs:bs + 512],
                    in_=wsrc[2048:2049, dc4 * 512:(dc4 + 1) * 512]),
                    pre_waits=(list(init_guards[dc4 % 2]) if dc4 < 2
                               else [last_mm[dc4 - 2]]))
                for nt in range(NT):
                    ps = psA if nt % 2 == 0 else psB
                    if nt < 2:
                        wait("te", wtok)
                        wait("te", btok)
                    if dc4 == 0 and nt == 0:
                        wait("te", x1_done)
                    pk = ev.get((dc4, nt - 2)) if nt >= 2 else \
                        (ev.get((dc4 - 1, NT - 2 + nt)) if dc4 > 0 else None)
                    if pk is not None:
                        wait("te", pk)
                    for k in range(NT):
                        te_(lambda e, k=k, nt=nt, buf=buf, ps=ps: e.matmul(
                            ps[:, 0:512],
                            Xt(k)[:, nt * 128:(nt + 1) * 128],
                            buf[:, k * 512:(k + 1) * 512],
                            start=(k == 0), stop=False))
                    mm = te(lambda e, bs=bs, ps=ps: e.matmul(
                        ps[:, 0:512], ONESR, BROW[0:1, bs:bs + 512],
                        start=False, stop=True))
                    last_mm[dc4] = mm
                    ev[(dc4, nt)] = evac_fn(dc4, nt, ps, mm)
                    if kv_hook is not None:
                        kv_hook(dc4, nt, ev[(dc4, nt)])
            return ev, last_mm

        def k_evac(dc4, nt, ps, mm):
            wait("sc", mm)
            if dc4 == 0 and nt == 0:
                wait("sc", tok_pp)
            return sc(lambda e, dc4=dc4, nt=nt, ps=ps: e.activation(
                Yt(nt)[:, dc4 * 512:(dc4 + 1) * 512], ps[:, 0:512],
                AF.Relu, scale=PP[:, 48 + nt:49 + nt]))

        k_ev, k_last = rm_phase512(wkTa, k_evac, [[x1_done], [x1_done]])
        if "t_K" in tap_out:
            store_toks.append(SQ.issue(lambda e: e.dma_start(
                out=tap_out["t_K"], in_=Y[:, :]), pre_waits=[k_ev[(3, 15)]]))

        psKV = [psP[2][:, 0:129], psP[3][:, 0:129],
                psP[4][:, 0:129], psP[5][:, 0:129]]
        kv_mm = {}
        kv_cp = {}

        def v_evac(dc4, nt, ps, mm):
            wait("sc", mm)
            sslot = nt % 2
            pk = kv_mm.get((dc4, nt - 2)) if nt >= 2 else \
                (kv_mm.get((dc4 - 1, NT - 2 + nt)) if dc4 > 0 else None)
            if pk is not None:
                wait("sc", pk)
            if dc4 == 0 and nt == 0:
                wait("sc", tok_ones)
            dst = VE2[:, sslot * 520:sslot * 520 + 516].rearrange(
                "p (j f) -> p j f", j=4)[:, :, 0:128]
            src = ps.ap()[:, 0:512].rearrange(
                "p (j f) -> p j f", j=4)[:, :, 0:128]
            return sc(lambda e, dst=dst, src=src: e.activation(
                dst, src, AF.Copy))

        def kv_hook(dc4, nt, evtok):
            sslot = nt % 2
            tok = None
            for j in range(4):
                hd = 4 * dc4 + j
                wait("te", evtok)
                wait("te", k_ev[(dc4, nt)])
                if nt == 0 and dc4 > 0:
                    pc = kv_cp.get((dc4 - 1, j))
                    if pc is not None:
                        wait("te", pc)
                tok = te(lambda e, j=j, sslot=sslot, hd=hd, nt=nt: e.matmul(
                    psKV[j], Yt(nt)[:, hd * 128:(hd + 1) * 128],
                    VE2[:, sslot * 520 + j * 129:sslot * 520 + j * 129 + 129],
                    start=(nt == 0), stop=(nt == NT - 1)))
            kv_mm[(dc4, nt)] = tok
            if nt == NT - 1:
                for j in range(4):
                    hd = 4 * dc4 + j
                    wait("ve", tok)
                    kv_cp[(dc4, j)] = ve(lambda e, j=j, hd=hd: e.tensor_copy(
                        OST[:, hd * 129:(hd + 1) * 129], psKV[j]))

        v_ev, v_last = rm_phase512(wvTa, v_evac, [[k_last[2]], [k_last[3]]],
                                   kv_hook=kv_hook)
        kv_done = kv_cp[(3, 3)]

        kv_out = SQ.issue(lambda e: e.dma_start(
            out=cc_in.ap(), in_=OST.ap()[:, 0:H * 129]), pre_waits=[kv_done])
        wait("gp", kv_out)
        ccv = CCS.nxt()
        em("gp", lambda e: e.collective_compute(
            "AllReduce", ALU.add, replica_groups=[list(range(NCORES))],
            ins=[cc_in.ap().opt()], outs=[cc_out.ap().opt()]
        ).then_inc(CCS.h, 1))
        cc_tok = (CCS.h, ccv)
        if "t_KV" in tap_out:
            store_toks.append(SQ.issue(lambda e: e.dma_start(
                out=tap_out["t_KV"], in_=OST.ap()[:, 0:H * 129]),
                pre_waits=[kv_done]))

        q_ev = {}
        q_sc = {}
        q_last = {}
        vlast = None
        for dc4 in range(4):
            buf = CH[dc4 % 2]
            pre = [v_last[2 + dc4]] if dc4 < 2 else [q_last[dc4 - 2]]
            qch = GQ.issue(lambda e, dc4=dc4, buf=buf: e.dma_start(
                out=buf.rearrange("p (k f) -> p k f", k=16),
                in_=wqT[:, dc4 * 512:(dc4 + 1) * 512].rearrange(
                    "(k p) f -> p k f", p=128)), pre_waits=pre)
            for dtl in range(4):
                dt = 4 * dc4 + dtl
                for c in range(4):
                    idx = dt * 4 + c
                    ps = psA if idx % 2 == 0 else psB
                    if dtl == 0 and c == 0:
                        wait("te", qch)
                    pk = q_sc.get(idx - 2)
                    if pk is not None:
                        wait("te", pk)
                    for k in range(NT - 1):
                        te_(lambda e, k=k, buf=buf, dtl=dtl, c=c, ps=ps:
                            e.matmul(
                                ps[:, 0:512],
                                buf[:, k * 512 + dtl * 128:
                                    k * 512 + (dtl + 1) * 128],
                                Xt(k)[:, c * 512:(c + 1) * 512],
                                start=(k == 0), stop=False))
                    mmq = te(lambda e, buf=buf, dtl=dtl, c=c, ps=ps: e.matmul(
                        ps[:, 0:512],
                        buf[:, (NT - 1) * 512 + dtl * 128:
                            (NT - 1) * 512 + (dtl + 1) * 128],
                        Xt(NT - 1)[:, c * 512:(c + 1) * 512],
                        start=False, stop=True))
                    q_last[dc4] = mmq
                    oslot = (idx % 2) * 512
                    wait("sc", mmq)
                    pv = q_ev.get(divmod(idx - 2, 4)) if idx >= 2 else None
                    if pv is not None:
                        wait("sc", pv)
                    if idx == 0:
                        wait("sc", kv_out)
                    q_sc[idx] = sc(lambda e, dt=dt, oslot=oslot, ps=ps:
                                   e.activation(
                                       OST[:, oslot:oslot + 512],
                                       ps[:, 0:512], AF.Relu,
                                       bias=PP[:, 0 + dt:1 + dt]))
                    wait("ve", q_sc[idx])
                    if idx == 0:
                        wait("ve", tok_dk)
                    q_ev[(dt, c)] = ve(lambda e, dt=dt, c=c, oslot=oslot:
                                       e.tensor_mul(
                                           Yt(dt)[:, c * 512:(c + 1) * 512],
                                           OST.ap().bitcast(F32)[
                                               :, oslot:oslot + 512]
                                           if False else
                                           OST[:, oslot:oslot + 512],
                                           DKB[:, c * 512:(c + 1) * 512]))
                    vlast = q_ev[(dt, c)]
        q_done = vlast
        if "t_QT" in tap_out:
            store_toks.append(SQ.issue(lambda e: e.dma_start(
                out=tap_out["t_QT"], in_=Y[:, :]), pre_waits=[q_done]))

        wait("gp", cc_tok)
        kvbf = GQ.issue(lambda e: e.dma_start(
            out=DKB[:, 0:H * 129], in_=cc_out.ap()), pre_waits=[q_done])

        psDen = lambda c: psP[4 + c][0:16, 0:512]
        gp_ksb = gp(lambda e: e.memset(KSB[:, :], 0.0))
        wait("sc", gp_ksb)
        wait("sc", kvbf)
        ksb_tok = None
        for k in range(NT):
            ksb_tok = sc(lambda e, k=k: e.activation(
                KSB[:, k * 16 + k:k * 16 + k + 1],
                DKB[:, k * 129 + 128:k * 129 + 129], AF.Copy))
        den_t = {}
        for c in range(4):
            for k in range(NT):
                if c == 0 and k == 0:
                    wait("te", ksb_tok)
                    wait("te", kvbf)
                den_t[c] = te(lambda e, c=c, k=k: e.matmul(
                    psDen(c), KSB[:, k * 16:(k + 1) * 16],
                    Yt(k)[:, c * 512:(c + 1) * 512],
                    start=(k == 0), stop=(k == NT - 1)))
        z_t = {}
        for c in range(4):
            wait("ve", den_t[c])
            ve_(lambda e, c=c: e.tensor_scalar_add(psDen(c), psDen(c),
                                                   EPS_ATTN))
            ve_(lambda e, c=c: e.reciprocal(psDen(c), psDen(c)))
            z_t[c] = ve(lambda e, c=c: e.tensor_copy(
                ZALL[:, c * 512:(c + 1) * 512], psDen(c)))
        att = {}
        zb_t = {}
        for hd in range(H):
            for c in range(4):
                idx = hd * 4 + c
                psZ = psP[2] if idx % 2 == 0 else psP[3]
                psAt = psP[0] if idx % 2 == 0 else psP[1]
                wait("te", z_t[c])
                if idx < 2:
                    wait("te", kv_cp[(3, 3)])
                    wait("te", q_sc[63])
                pze = zb_t.get(divmod(idx - 2, 4)) if idx >= 2 else None
                if pze is not None:
                    wait("te", pze)
                ztok = te(lambda e, hd=hd, c=c, psZ=psZ: e.matmul(
                    psZ[:, 0:512], INDH(hd), ZALL[:, c * 512:(c + 1) * 512],
                    start=True, stop=True))
                pae = att.get(divmod(idx - 2, 4)) if idx >= 2 else None
                if pae is not None:
                    wait("te", pae)
                atok = te(lambda e, hd=hd, c=c, psAt=psAt: e.matmul(
                    psAt[:, 0:512], DKB[:, hd * 129:hd * 129 + 128],
                    Yt(hd)[:, c * 512:(c + 1) * 512],
                    start=True, stop=True))
                zdst = BC if idx % 2 == 0 else BC2
                wait("sc", ztok)
                zb_t[(hd, c)] = sc(lambda e, c=c, psZ=psZ, zdst=zdst:
                                   e.activation(zdst[:, c * 512:(c + 1) * 512],
                                                psZ[:, 0:512], AF.Copy))
                wait("ve", zb_t[(hd, c)])
                wait("ve", atok)
                att[(hd, c)] = ve(lambda e, hd=hd, c=c, psAt=psAt, zdst=zdst:
                                  e.tensor_mul(
                                      Xt(hd)[:, c * 512:(c + 1) * 512],
                                      psAt[:, 0:512],
                                      zdst[:, c * 512:(c + 1) * 512]))
        attn_done = att[(H - 1, 3)]
        if "t_attnT" in tap_out:
            store_toks.append(SQ.issue(lambda e: e.dma_start(
                out=tap_out["t_attnT"], in_=X[:, :]), pre_waits=[attn_done]))

        psS = {(0, 0): psP[4][0:1, 0:512], (0, 1): psP[6][0:1, 0:512],
               (1, 0): psP[5][0:1, 0:512], (1, 1): psP[7][0:1, 0:512]}
        bc2_toks = []
        last_sqstat = None
        sqstat_hist = {}
        pdma8 = {0: [], 1: []}
        wo_sc = {}
        wlast = {}
        p8vm = None
        SQTs = [VE2[:, 0:512], VE2[:, 512:1024]]
        for nh in range(2):
            st_toks = {}
            for dc4 in range(4):
                buf = CH[dc4 % 2]
                if nh == 0 and dc4 < 2:
                    pre = [q_last[2 + dc4]]
                elif dc4 < 2:
                    pre = [wlast[(0, 2 + dc4)]]
                else:
                    pre = [wlast[(nh, dc4 - 2)]]
                wch = GQ.issue(lambda e, dc4=dc4, buf=buf: e.dma_start(
                    out=buf.rearrange("p (k f) -> p k f", k=16),
                    in_=woT[:, dc4 * 512:(dc4 + 1) * 512].rearrange(
                        "(k p) f -> p k f", p=128)), pre_waits=pre)
                for dtl in range(4):
                    dt = 4 * dc4 + dtl
                    for cc in range(2):
                        idx = dt * 2 + cc
                        ps = psA if idx % 2 == 0 else psB
                        if dtl == 0 and cc == 0:
                            wait("te", wch)
                        if nh == 0 and idx == 0:
                            wait("te", attn_done)
                        pk = wo_sc.get((nh, idx - 2)) if idx >= 2 else \
                            (wo_sc.get((0, 30 + idx)) if nh == 1 else None)
                        if pk is not None:
                            wait("te", pk)
                        if nh == 1 and idx == 0 and p8vm is not None:
                            wait("te", p8vm)
                        SQT = SQTs[idx % 2]
                        for k in range(NT - 1):
                            te_(lambda e, k=k, buf=buf, dtl=dtl, cc=cc, nh=nh,
                                ps=ps: e.matmul(
                                    ps[:, 0:512],
                                    buf[:, k * 512 + dtl * 128:
                                        k * 512 + (dtl + 1) * 128],
                                    Xt(k)[:, nh * 1024 + cc * 512:
                                          nh * 1024 + (cc + 1) * 512],
                                    start=(k == 0), stop=False))
                        mmt = te(lambda e, buf=buf, dtl=dtl, cc=cc, nh=nh,
                                 ps=ps: e.matmul(
                                     ps[:, 0:512],
                                     buf[:, (NT - 1) * 512 + dtl * 128:
                                         (NT - 1) * 512 + (dtl + 1) * 128],
                                     Xt(NT - 1)[:, nh * 1024 + cc * 512:
                                           nh * 1024 + (cc + 1) * 512],
                                     start=False, stop=True))
                        wlast[(nh, dc4)] = mmt
                        wait("sc", mmt)
                        ev = sc(lambda e, dt=dt, cc=cc, nh=nh, ps=ps:
                                e.activation(
                                    Yt(dt)[:, nh * 1024 + cc * 512:
                                           nh * 1024 + (cc + 1) * 512],
                                    ps[:, 0:512], AF.Identity,
                                    bias=PP[:, 16 + dt:17 + dt]))
                        pq = sqstat_hist.get((nh, idx - 2)) if idx >= 2 else \
                            (sqstat_hist.get((0, 30 + idx)) if nh == 1
                             else None)
                        if pq is not None:
                            wait("sc", pq)
                        sqk = sc(lambda e, dt=dt, cc=cc, nh=nh, SQT=SQT:
                                 e.activation(
                                     SQT,
                                     Yt(dt)[:, nh * 1024 + cc * 512:
                                            nh * 1024 + (cc + 1) * 512],
                                     AF.Square))
                        wo_sc[(nh, idx)] = sqk
                        wait("te", ev)
                        wait("te", sqk)
                        te_(lambda e, cc=cc, dt=dt, nh=nh: e.matmul(
                            psS[(0, cc)], ONESC,
                            Yt(dt)[:, nh * 1024 + cc * 512:
                                   nh * 1024 + (cc + 1) * 512],
                            start=(dt == 0), stop=(dt == NT - 1)))
                        st_toks[cc] = te(lambda e, cc=cc, dt=dt, SQT=SQT:
                                         e.matmul(
                                             psS[(1, cc)], ONESC, SQT,
                                             start=(dt == 0),
                                             stop=(dt == NT - 1)))
                        sqstat_hist[(nh, idx)] = st_toks[cc]
                        last_sqstat = st_toks[cc]
            for cc in range(2):
                gc = nh * 1024 + cc * 512
                gpar = (nh * 2 + cc) % 2
                b0, b1, tm, ts, vmx = ln_rows(psS[(0, cc)], psS[(1, cc)],
                                              gc, st_toks[cc], pdma8[gpar],
                                              rb=(None if gpar == 0
                                                  else OROW))
                pdma8[gpar] = [tm, ts]
                bc2_toks += [b0, b1]
                p8vm = vmx
        if "t_out2T" in tap_out:
            store_toks.append(SQ.issue(lambda e: e.dma_start(
                out=tap_out["t_out2T"], in_=Y[:, :]),
                pre_waits=[wo_sc[(1, 31)]]))

        psS1b = lambda c: psP[c][0:1, 0:512]
        psS2b = lambda c: psP[4 + c][0:1, 0:512]
        p9_ve = {}
        p9_st = {}
        p9_sc = {}
        last_stat9 = None
        for t in range(NT):
            sl = t % 2
            pre = [p9_ve[t - 2]] if t >= 2 else \
                [kv_out, wlast[(1, 3)], wlast[(1, 2)]]
            ld = (SQ if sl == 0 else GQ).issue(
                lambda e, t=t, sl=sl: e.dma_start(
                    out=XT[:, sl * 2048:(sl + 1) * 2048],
                    in_=xT[t * 128:(t + 1) * 128, :]), pre_waits=pre)
            wait("gp", ld)
            if t == 0:
                for b in bc2_toks:
                    wait("gp", b)
            xbslot = WB_F[1 + sl][:, 0:2048] if sl == 1 else \
                WB_F[1]
            if t >= 2:
                wait("gp", p9_st[t - 2])
                wait("gp", p9_sc[t - 2])
            tmp = TMPF if t % 2 == 0 else OST.ap()[:, 0:2048]
            if t >= 2:
                wait("gp", p9_ve[t - 2])
            if t == 0:
                wait("gp", wlast[(1, 2)])
                wait("gp", q_done)
            g9 = gp(lambda e, t=t, tmp=tmp: e.tensor_sub(tmp, Yt(t),
                                                         BC2[:, :]))
            wait("ve", g9)
            ve_(lambda e, tmp=tmp: e.tensor_mul(tmp, tmp, BC[:, :]))
            p9_ve[t] = ve(lambda e, sl=sl, xbslot=xbslot, tmp=tmp:
                          e.tensor_add(
                              xbslot, tmp,
                              XT[:, sl * 2048:(sl + 1) * 2048]))
            p9_st[t] = SQ.issue(lambda e, t=t, xbslot=xbslot: e.dma_start(
                out=xb_dram[t * 128:(t + 1) * 128, :], in_=xbslot),
                pre_waits=[p9_ve[t]])
            wait("sc", p9_ve[t])
            p9_sc[t] = sc(lambda e, t=t, xbslot=xbslot: e.activation(
                Xt(t), xbslot, AF.Copy))
            sq9 = sc(lambda e, t=t: e.activation(Yt(t), Xt(t), AF.Square))
            wait("te", p9_sc[t])
            wait("te", sq9)
            for c in range(4):
                te_(lambda e, t=t, c=c: e.matmul(
                    psS1b(c), ONESC, Xt(t)[:, c * 512:(c + 1) * 512],
                    start=(t == 0), stop=(t == NT - 1)))
                last_stat9 = te(lambda e, t=t, c=c: e.matmul(
                    psS2b(c), ONESC, Yt(t)[:, c * 512:(c + 1) * 512],
                    start=(t == 0), stop=(t == NT - 1)))
        if "t_xB" in tap_out:
            store_toks.append(SQ.issue(lambda e: e.dma_start(
                out=tap_out["t_xB"], in_=xb_dram),
                pre_waits=[p9_st[NT - 1]]))
        bc3 = []
        pdma = {0: [], 1: []}
        for c in range(4):
            b0, b1, tm, ts, _ = ln_rows(psS1b(c), psS2b(c), c * 512,
                                        last_stat9, pdma[c % 2],
                                        rb=(None if c % 2 == 0 else OROW))
            pdma[c % 2] = [tm, ts]
            bc3 += [b0, b1]

        x3_toks = [None] * NT
        vm10 = None
        for t in range(NT):
            tmp = TMPF if t % 2 == 0 else OST.ap()[:, 0:2048]
            if t % 2 == 0:
                if t == 0:
                    for b in bc3:
                        wait("gp", b)
                    wait("gp", tok_scales)
                if t >= 2:
                    wait("gp", x3_toks[t - 2])
                g10 = gp(lambda e, t=t, tmp=tmp: e.tensor_sub(tmp, Xt(t),
                                                              BC2[:, :]))
                wait("ve", g10)
            else:
                if t == 1:
                    for b in bc3:
                        wait("ve", b)
                if t >= 2:
                    wait("ve", x3_toks[t - 2])
                ve_(lambda e, t=t, tmp=tmp: e.tensor_sub(tmp, Xt(t),
                                                         BC2[:, :]))
            vm10 = ve(lambda e, tmp=tmp: e.tensor_mul(tmp, tmp, BC[:, :]))
            wait("sc", vm10)
            x3_toks[t] = sc(lambda e, t=t, tmp=tmp: e.activation(
                Yt(t), tmp, AF.Copy, scale=PP[:, 32 + t:33 + t]))
        x3_done = x3_toks[NT - 1]
        if "t_x3sT" in tap_out:
            store_toks.append(SQ.issue(lambda e: e.dma_start(
                out=tap_out["t_x3sT"], in_=Y[:, :]), pre_waits=[x3_done]))

        h128_ev = {}
        for t in range(NT):
            for c in range(4):
                idx = t * 4 + c
                ps = psP[0] if idx % 2 == 0 else psP[1]
                wait("te", x3_toks[t])
                if idx == 0:
                    wait("te", last_stat9)
                pk = h128_ev.get(idx - 2)
                if pk is not None:
                    wait("te", pk)
                mt = te(lambda e, t=t, c=c, ps=ps: e.matmul(
                    ps[:, 0:512], H128, Yt(t)[:, c * 512:(c + 1) * 512],
                    start=True, stop=True))
                wait("sc", mt)
                h128_ev[idx] = sc(lambda e, t=t, c=c, ps=ps: e.activation(
                    Xt(t)[:, c * 512:(c + 1) * 512], ps[:, 0:512], AF.Copy))
        h128_done = h128_ev[NT * 4 - 1]
        prev_level = {a: h128_done for a in range(NT)}
        for lvl in range(4):
            bit = 1 << lvl
            src, dst = (X, Y) if lvl % 2 == 0 else (Y, X)
            cur = {}
            for a in range(NT):
                b_ = a ^ bit
                wait("ve", prev_level[min(a, b_)])
                wait("ve", prev_level[max(a, b_)])
                if a < b_:
                    cur[a] = ve(lambda e, a=a, b_=b_, src=src, dst=dst:
                                e.tensor_add(dst[:, a * 2048:(a + 1) * 2048],
                                             src[:, a * 2048:(a + 1) * 2048],
                                             src[:, b_ * 2048:(b_ + 1) * 2048]))
                else:
                    cur[a] = ve(lambda e, a=a, b_=b_, src=src, dst=dst:
                                e.tensor_sub(dst[:, a * 2048:(a + 1) * 2048],
                                             src[:, b_ * 2048:(b_ + 1) * 2048],
                                             src[:, a * 2048:(a + 1) * 2048]))
            prev_level = cur
        fT_done = prev_level[NT - 1]
        if "t_fT" in tap_out:
            store_toks.append(SQ.issue(lambda e: e.dma_start(
                out=tap_out["t_fT"], in_=X[:, :]), pre_waits=[fT_done]))

        wdl = []
        for g in range(4):
            wdl.append(GQ.issue(lambda e, g=g: e.dma_start(
                out=Y[:, 16384 + g * 4096:16384 + (g + 1) * 4096].rearrange(
                    "p (k f) -> p k f", k=16),
                in_=wd_slice[g * 2048:(g + 1) * 2048, :].rearrange(
                    "(k p) f -> p k f", p=128)), pre_waits=[fT_done]))
        YA = lambda k: Y[:, k * 256:(k + 1) * 256]
        YB = lambda k: Y[:, 16384 + k * 256:16384 + (k + 1) * 256]
        wde = {}
        for k in range(ITI):
            ps = psP[2] if k % 2 == 0 else psP[3]
            if k % 16 == 0:
                wait("te", wdl[k // 16])
            if k == 0:
                wait("te", attn_done)
                wait("te", zb_t[(H - 1, 3)])
            pk = wde.get(k - 2)
            if pk is not None:
                wait("te", pk)
            mt = te(lambda e, k=k, ps=ps: e.matmul(
                ps[:, 0:256], H128, YB(k), start=True, stop=True))
            wait("sc", mt)
            wde[k] = sc(lambda e, k=k, ps=ps: e.activation(
                YA(k), ps[:, 0:256], AF.Copy))
        prevw = {a: wde[a] for a in range(ITI)}
        for lvl in range(6):
            bit = 1 << lvl
            srcf, dstf = (YA, YB) if lvl % 2 == 0 else (YB, YA)
            cur = {}
            for a in range(ITI):
                b_ = a ^ bit
                wait("ve", prevw[min(a, b_)])
                wait("ve", prevw[max(a, b_)])
                if a < b_:
                    cur[a] = ve(lambda e, a=a, b_=b_, srcf=srcf, dstf=dstf:
                                e.tensor_add(dstf(a), srcf(a), srcf(b_)))
                else:
                    cur[a] = ve(lambda e, a=a, b_=b_, srcf=srcf, dstf=dstf:
                                e.tensor_sub(dstf(a), srcf(b_), srcf(a)))
            prevw = cur
        wcomb_done = prevw[ITI - 1]
        contrib_toks = []
        for g in range(4):
            for dtl in range(2):
                contrib_toks.append(SQ.issue(lambda e, g=g, dtl=dtl:
                                             e.dma_start(
                    out=wd_contrib.ap().rearrange(
                        "(dtl k p) f -> p dtl k f", dtl=2, k=64)[
                        :, dtl, g * 16:(g + 1) * 16, :],
                    in_=Y[:, g * 4096:(g + 1) * 4096]
                    .rearrange("p (k g2 f) -> p g2 k f", k=16, g2=2)[
                        :, dtl, :, :]),
                    pre_waits=[wcomb_done]))
        ag_tok = None

        hst = [BC, BC2]
        hstore = {}
        hmul = {}
        uge_t = {}
        uge_half = {}
        ulast = {}
        p9_wb1 = p9_st[14]
        p9_wb2 = p9_st[15]
        for ic in range(32):
            bu = ic % 2
            if ic == 0:
                pre_u = [x3_done]
            elif ic == 1:
                pre_u = [p9_wb1]
            else:
                pre_u = [ulast[ic - 2]]
            utok = GQ.issue(lambda e, ic=ic, bu=bu: e.dma_start(
                out=WB[bu].rearrange("p (k f) -> p k f", k=16),
                in_=wupT[:, ic * 256:(ic + 1) * 256].rearrange(
                    "(k p) f -> p k f", p=128)), pre_waits=pre_u)
            guse_wb2 = (ic % 2 == 0)
            if ic == 0:
                pre_g = [p9_wb2]
            elif ic == 1:
                pre_g = [x3_done]
            else:
                pre_g = [ulast[ic - 2]]
            if guse_wb2:
                gtok = GQ.issue(lambda e, ic=ic: e.dma_start(
                    out=WB[2].rearrange("p (k f) -> p k f", k=16),
                    in_=wgT[:, ic * 256:(ic + 1) * 256].rearrange(
                        "(k p) f -> p k f", p=128)), pre_waits=pre_g)
            else:
                gtok = GQ.issue(lambda e, ic=ic: e.dma_start(
                    out=UGEHI.rearrange("p (k f) -> p k f", k=16),
                    in_=wgT[:, ic * 256:(ic + 1) * 256].rearrange(
                        "(k p) f -> p k f", p=128)), pre_waits=pre_g)
            if ic == 4:
                for ct in contrib_toks:
                    wait("gp", ct)
                agv = AGS.nxt()
                em("gp", lambda e: e.collective_compute(
                    "AllGather", ALU.bypass,
                    replica_groups=[list(range(NCORES))],
                    ins=[wd_contrib.ap().opt()], outs=[wdc.ap().opt()]
                ).then_inc(AGS.h, 1))
                ag_tok = (AGS.h, agv)
            for itl in range(2):
                it = 2 * ic + itl
                hs = hst[it % 2]
                if itl == 0:
                    wait("te", utok)
                    wait("te", gtok)
                for hv in (0, 1):
                    cs = (2 * hv, 2 * hv + 1)
                    if it == 0 and hv == 0:
                        wait("te", fT_done)
                    if it == 0:
                        wait("te", wde[ITI - 1])
                    pu = uge_half.get((it - 1, hv))
                    if pu is not None:
                        wait("te", pu)
                    mmug = None
                    for k in range(NT):
                        for c in cs:
                            te_(lambda e, k=k, c=c, bu=bu, itl=itl: e.matmul(
                                psP[c][:, 0:512],
                                WB[bu][:, k * 256 + itl * 128:
                                      k * 256 + (itl + 1) * 128],
                                Xt(k)[:, c * 512:(c + 1) * 512],
                                start=(k == 0), stop=(k == NT - 1)))
                        for c in cs:
                            if guse_wb2:
                                lhs = WB[2][:, k * 256 + itl * 128:
                                            k * 256 + (itl + 1) * 128]
                            else:
                                lhs = UGEHI[:, k * 256 + itl * 128:
                                            k * 256 + (itl + 1) * 128]
                            if k == NT - 1 and c == cs[1]:
                                mmug = te(lambda e, lhs=lhs, c=c: e.matmul(
                                    psP[4 + c][:, 0:512], lhs,
                                    Xt(NT - 1)[:, c * 512:(c + 1) * 512],
                                    start=False, stop=True))
                            else:
                                te_(lambda e, k=k, c=c, lhs=lhs: e.matmul(
                                    psP[4 + c][:, 0:512], lhs,
                                    Xt(k)[:, c * 512:(c + 1) * 512],
                                    start=(k == 0), stop=(k == NT - 1)))
                    if hv == 1:
                        ulast[ic] = mmug
                    wait("sc", mmug)
                    pvm = hmul.get((it - 1, cs[1]))
                    if pvm is not None:
                        wait("sc", pvm)
                    ug2 = None
                    for c in cs:
                        sc_(lambda e, it=it, c=c: e.activation(
                            UGE[:, c * 512:(c + 1) * 512], psP[c][:, 0:512],
                            AF.Copy, scale=PP[:, 64 + it:65 + it]))
                        ug2 = sc(lambda e, it=it, c=c: e.activation(
                            UGE[:, 2048 + c * 512:2048 + (c + 1) * 512],
                            psP[4 + c][:, 0:512],
                            AF.Silu, scale=PP[:, 128 + it:129 + it]))
                    uge_half[(it, hv)] = ug2
                    for c in cs:
                        wait("ve", uge_half[(it, hv)])
                        if c == cs[0]:
                            pst = hstore.get(it - 2)
                            if pst is not None:
                                wait("ve", pst)
                        hmul[(it, c)] = ve(lambda e, c=c, hs=hs: e.tensor_mul(
                            hs[:, c * 512:(c + 1) * 512],
                            UGE[:, c * 512:(c + 1) * 512],
                            UGE[:, 2048 + c * 512:2048 + (c + 1) * 512]))
                uge_t[it] = uge_half[(it, 1)]
                hstore[it] = SQ.issue(lambda e, it=it, hs=hs: e.dma_start(
                    out=h_dram[it * 128:(it + 1) * 128, :], in_=hs[:, :]),
                    pre_waits=[hmul[(it, 3)]])
        h_done = hstore[63]
        up_evac_last = uge_t[63]
        if "t_h" in tap_out:
            store_toks.append(SQ.issue(lambda e: e.dma_start(
                out=tap_out["t_h"], in_=h_dram), pre_waits=[h_done]))
        if "t_wdc" in tap_out:
            store_toks.append(SQ.issue(lambda e: e.dma_start(
                out=tap_out["t_wdc"], in_=wdc.ap()[0:16384, :]),
                pre_waits=[ag_tok]))

        psq1 = psP[4][0:1, 0:512]
        psq2 = psP[5][0:1, 0:512]
        snap = [(s_.h, s_.c) for s_ in SQ.sems + GQ.sems if s_.c > 0]
        out_toks = []
        dn_ev = {}
        wlast12 = {}
        ln4vm = {}
        fin_ve = {}
        last_qstat = None
        pd12 = []
        YR = lambda dt, kk: Y[:, (dt % 4) * 8192 + kk * 128:
                              (dt % 4) * 8192 + (kk + 1) * 128]
        xbt_toks = {}

        def emit_xbt(q, t):
            prex = [p9_st[t]]
            if q == 0:
                prex.append(up_evac_last)
                if t >= 8:
                    prex.append(fin_ve[(0, t - 8)])
            else:
                if t >= 8:
                    prex.append(fin_ve[(q, t - 8)])
                else:
                    prex.append(fin_ve[(q - 1, 8 + t)])
            xbt_toks[(q, t)] = SQ.issue(lambda e, q=q, t=t: e.dma_start(
                out=WBLF[:, (t % 8) * 512:(t % 8 + 1) * 512],
                in_=xb_dram[t * 128:(t + 1) * 128,
                            q * 512:(q + 1) * 512]), pre_waits=prex)

        def emit_finals(q, b0, b1):
            for t in range(NT):
                osl = (t % 2) * 512
                if (q, t) not in xbt_toks:
                    emit_xbt(q, t)
                xbt = xbt_toks[(q, t)]
                wait("ve", b0)
                wait("ve", b1)
                wait("ve", xbt)
                wait("ve", dn_ev[(q, t)])
                if len(out_toks) >= 2:
                    wait("ve", out_toks[-2])
                ve_(lambda e, t=t, osl=osl, q=q: e.tensor_sub(
                    OST[:, osl:osl + 512], UGE[:, t * 512:(t + 1) * 512],
                    BC2[:, q * 512:(q + 1) * 512]))
                ve_(lambda e, osl=osl, q=q: e.tensor_mul(
                    OST[:, osl:osl + 512], OST[:, osl:osl + 512],
                    BC[:, q * 512:(q + 1) * 512]))
                fin_ve[(q, t)] = ve(lambda e, t=t, osl=osl: e.tensor_add(
                    OST[:, osl:osl + 512], OST[:, osl:osl + 512],
                    WBLF[:, (t % 8) * 512:(t % 8 + 1) * 512]))
                out_toks.append(SQ.issue(
                    lambda e, q=q, t=t, osl=osl: e.dma_start(
                        out=yT[t * 128:(t + 1) * 128, q * 512:(q + 1) * 512],
                        in_=OST[:, osl:osl + 512]),
                    pre_waits=[fin_ve[(q, t)]]))

        pending_finals = None
        ring_toks = {}
        dt15_prog = {}

        def emit_ring(q, dt):
            if q == 0 and dt < 4:
                pre0 = [ag_tok, wcomb_done, fT_done]
                if tap_out:
                    pre0 += list(snap)
            elif dt >= 4:
                pre0 = [wlast12[(q, dt - 4)]]
            else:
                pre0 = [wlast12[(q - 1, 12 + dt)]]
            t0 = GQ.issue(lambda e, dt=dt: e.dma_start(
                out=Y[:, (dt % 4) * 8192:(dt % 4) * 8192 + 4096]
                .rearrange("p (k f) -> p k f", k=32),
                in_=wdc.ap()[dt * 8192:dt * 8192 + 4096, :].rearrange(
                    "(k p) f -> p k f", p=128)), pre_waits=pre0)
            t1 = GQ.issue(lambda e, dt=dt: e.dma_start(
                out=Y[:, (dt % 4) * 8192 + 4096:(dt % 4) * 8192 + 8192]
                .rearrange("p (k f) -> p k f", k=32),
                in_=wdc.ap()[dt * 8192 + 4096:(dt + 1) * 8192, :]
                .rearrange("(k p) f -> p k f", p=128)),
                pre_waits=list(pre0))
            ring_toks[(q, dt)] = (t0, t1)

        for q in range(4):
            for dt in range(4):
                emit_ring(q, dt)
            hq_toks = []
            for g in range(8):
                if q == 0:
                    pre = [up_evac_last] + [hstore[g * 8 + j]
                                            for j in range(8)]
                    if tap_out:
                        pre += list(snap)
                else:
                    pre = [dt15_prog[(q - 1, g)]]
                QQ = SQ if g % 2 == 0 else GQ
                hq_toks.append(QQ.issue(lambda e, q=q, g=g: e.dma_start(
                    out=X.ap().rearrange("p (i f) -> p i f", i=ITI)[
                        :, g * 8:(g + 1) * 8, :],
                    in_=h_dram.rearrange("(i p) f -> p i f", p=128)[
                        :, g * 8:(g + 1) * 8, q * 512:(q + 1) * 512]),
                    pre_waits=pre))
            if pending_finals is not None:
                emit_finals(*pending_finals)
                pending_finals = None
            if q == 3:
                for t in range(8):
                    emit_xbt(3, t)
            q_stat = None
            for dt in range(16):
                ps = psP[0] if dt % 2 == 0 else psP[1]
                t0, t1 = ring_toks[(q, dt)]
                if dt == 0 and q == 0:
                    wait("te", up_evac_last)
                pde = dn_ev.get((q, dt - 2)) if dt >= 2 else \
                    (dn_ev.get((q - 1, 14 + dt)) if q > 0 else None)
                if pde is not None:
                    wait("te", pde)
                wait("te", t0)
                mmd = None
                for k in range(ITI):
                    if k == 32:
                        wait("te", t1)
                    if dt == 0 and k % 8 == 0:
                        wait("te", hq_toks[k // 8])
                    last = (k == ITI - 1)
                    tokd = (dt == 15 and k % 8 == 7) or last
                    if tokd:
                        mmd = te(lambda e, k=k, dt=dt, ps=ps, last=last:
                                 e.matmul(ps[:, 0:512], YR(dt, k), Xq(k),
                                          start=(k == 0), stop=last))
                        if dt == 15:
                            dt15_prog[(q, k // 8)] = mmd
                    else:
                        te_(lambda e, k=k, dt=dt, ps=ps: e.matmul(
                            ps[:, 0:512], YR(dt, k), Xq(k),
                            start=(k == 0), stop=False))
                wlast12[(q, dt)] = mmd
                if dt + 4 < 16:
                    emit_ring(q, dt + 4)
                wait("sc", mmd)
                if q == 0 and dt == 0:
                    wait("sc", hmul[(63, 3)])
                pfin = fin_ve.get((q - 1, dt)) if q > 0 else None
                if pfin is not None:
                    wait("sc", pfin)
                ev = sc(lambda e, dt=dt, ps=ps: e.activation(
                    UGE[:, dt * 512:(dt + 1) * 512], ps[:, 0:512],
                    AF.Copy, scale=PP[:, 192 + dt:193 + dt]))
                dn_ev[(q, dt)] = ev
                if last_qstat is not None:
                    wait("sc", last_qstat)
                sq12 = sc(lambda e, dt=dt: e.activation(
                    VE2[:, 0:512], UGE[:, dt * 512:(dt + 1) * 512],
                    AF.Square))
                wait("te", ev)
                wait("te", sq12)
                if dt == 0 and q > 0:
                    wait("te", ln4vm[q - 1])
                te_(lambda e, dt=dt: e.matmul(
                    psq1, ONESC, UGE[:, dt * 512:(dt + 1) * 512],
                    start=(dt == 0), stop=(dt == NT - 1)))
                q_stat = te(lambda e, dt=dt: e.matmul(
                    psq2, ONESC, VE2[:, 0:512],
                    start=(dt == 0), stop=(dt == NT - 1)))
                last_qstat = q_stat
            if q == 0 and "t_x5" in tap_out:
                store_toks.append(SQ.issue(lambda e: e.dma_start(
                    out=tap_out["t_x5"], in_=UGE[:, 0:NT * 512]),
                    pre_waits=[dn_ev[(0, 15)]]))
            pd4 = ([h_done] if q == 0 else []) + pd12
            b0, b1, tm4, ts4, vm4 = ln_rows(psq1, psq2, q * 512, q_stat, pd4)
            pd12 = [tm4, ts4]
            ln4vm[q] = vm4
            pending_finals = (q, b0, b1)
        emit_finals(*pending_finals)

        for s_ in SQ.sems + GQ.sems:
            if s_.c > 0:
                wait("sync", (s_.h, s_.c))

        with nc.Block() as block:
            @block.sync
            def _(eng):
                for fn in P.streams["sync"]:
                    fn(eng)

            @block.gpsimd
            def _(eng):
                for fn in P.streams["gp"]:
                    fn(eng)

            @block.scalar
            def _(eng):
                for fn in P.streams["sc"]:
                    fn(eng)

            @block.vector
            def _(eng):
                for fn in P.streams["ve"]:
                    fn(eng)

            @block.tensor
            def _(eng):
                for fn in P.streams["te"]:
                    fn(eng)

    return nc


def _host_inputs(inputs):
    x = np.asarray(inputs["L0_attn_x"])
    Wq = np.asarray(inputs["L0_attn_Wq"]); bq = np.asarray(inputs["L0_attn_bq"])
    Wk = np.asarray(inputs["L0_attn_Wk"]); bk = np.asarray(inputs["L0_attn_bk"])
    Wv = np.asarray(inputs["L0_attn_Wv"]); bv = np.asarray(inputs["L0_attn_bv"])
    Wo = np.asarray(inputs["L0_attn_Wo"]); bo = np.asarray(inputs["L0_attn_bo"])
    su = np.asarray(inputs["L0_ffn_SU_ug"])
    sd = np.asarray(inputs["L0_ffn_SU_dp"])
    wsug = np.asarray(inputs["L0_ffn_Wscale_ug"])
    wsdp = np.asarray(inputs["L0_ffn_Wscale_dp"])
    Wup = np.asarray(inputs["L0_ffn_W_up"])
    Wg = np.asarray(inputs["L0_ffn_W_gate"])
    Wd = np.asarray(inputs["L0_ffn_W_down"])

    f = np.float32
    wqTv = np.ascontiguousarray(Wq.T, dtype=f)
    wkTav = np.ascontiguousarray(
        np.concatenate([Wk.T, bk[None, :]], axis=0), dtype=f)
    wvTav = np.ascontiguousarray(
        np.concatenate([Wv.T, bv[None, :]], axis=0), dtype=f)
    woTv = np.ascontiguousarray(Wo.T, dtype=f)
    wupTv = np.ascontiguousarray(Wup.T, dtype=f)
    wgTv = np.ascontiguousarray(Wg.T, dtype=f)
    wdTv = np.ascontiguousarray(Wd.T, dtype=f)

    pp = np.zeros((128, 208), dtype=f)
    pp[:, 0:16] = bq.reshape(16, 128).T
    pp[:, 16:32] = bo.reshape(16, 128).T
    pp[:, 32:48] = su.reshape(16, 128).T
    pp[:, 64:128] = wsug[:INTER].reshape(64, 128).T
    pp[:, 128:192] = wsug[INTER:].reshape(64, 128).T
    pp[:, 192:208] = wsdp.reshape(16, 128).T
    sd_pp = np.ascontiguousarray(sd.reshape(64, 128).T, dtype=f)

    bidx = np.arange(128, dtype=np.uint32)
    par = np.bitwise_count(bidx[:, None] & bidx[None, :]) % 2
    h128 = np.where(par == 0, 1.0, -1.0).astype(ml_dtypes.bfloat16)
    consts = np.zeros((128, 2308), dtype=ml_dtypes.bfloat16)
    consts[:, 0:128] = h128
    consts[:, 128] = 1.0
    consts[0, 129:257] = 1.0
    for hh in range(16):
        consts[hh, 260 + hh * 128:260 + (hh + 1) * 128] = 1.0

    pos = np.arange(16384, dtype=np.float64)
    decay_full = np.exp(-pos / 16384.0)

    in_maps = []
    for c in range(NCORES):
        xTc = np.ascontiguousarray(x[0, c * N:(c + 1) * N, :].T, dtype=f)
        dslice = decay_full[c * N:(c + 1) * N]
        ppc = pp.copy()
        ppc[:, 48:64] = dslice.reshape(16, 128).T.astype(f)
        decay_bc = np.ascontiguousarray(np.broadcast_to(
            dslice.astype(ml_dtypes.bfloat16)[None, :], (128, N)))
        wd_sl = np.ascontiguousarray(wdTv[:, c * 256:(c + 1) * 256])
        in_maps.append({
            "xT": xTc, "wqT": wqTv, "wkTa": wkTav, "wvTa": wvTav, "woT": woTv,
            "wupT": wupTv, "wgT": wgTv, "wd_slice": wd_sl,
            "pp_in": ppc, "sd_in": sd_pp, "consts_in": consts,
            "decay_in": decay_bc,
        })
    return in_maps


def kernel(**inputs):
    in_maps = _host_inputs(inputs)
    nc = build_nc()
    res = run_bass_kernel_spmd(nc, in_maps, core_ids=list(range(NCORES)))
    out = np.empty((1, 16384, D), dtype=np.float32)
    for c in range(NCORES):
        out[0, c * N:(c + 1) * N, :] = np.asarray(res.results[c]["yT"]).T
    return out
